# version 24
# baseline (speedup 1.0000x reference)
import numpy as np
import ml_dtypes
import concourse.bass as bass
import concourse.mybir as mybir
from concourse.bass_utils import run_bass_kernel_spmd

F32 = mybir.dt.float32
BF16 = mybir.dt.bfloat16
AF = mybir.ActivationFunctionType
ALU = mybir.AluOpType
AX = mybir.AxisListType

D = 1024
NH = 16
HD = 64
FF = 768
NE = 8
ATT_B = 10.0
EPS_LN = 1e-5
EPS_MOE = 1e-20
BIG = 1e30
ENGINES = ("tensor", "vector", "scalar", "gpsimd", "sync")


class Prog:
    def __init__(self, nc):
        self.nc = nc
        self.streams = {e: [] for e in ENGINES}

    def emit(self, eng, fn):
        self.streams[eng].append(fn)

    def flush(self, name=None):
        if not any(self.streams.values()):
            return
        with self.nc.Block(name) as blk:
            for eng in ENGINES:
                fns = self.streams[eng]
                if not fns:
                    continue

                def runner(e, fns=fns):
                    for fn in fns:
                        fn(e)
                getattr(blk, eng)(runner)
        self.streams = {e: [] for e in ENGINES}


class Sem:

    def __init__(self, nc, name):
        self.h = nc.alloc_semaphore(name)
        self.n = 0

    def inc(self, k=1):
        self.n += k
        h = self.h

        def attach(instr):
            instr.then_inc(h, k)
        return attach, self.n


class Slot:
    def __init__(self, n):
        self.n = n
        self.waits = [[] for _ in range(n)]

    def acquire(self, i):
        return i % self.n, list(self.waits[i % self.n])

    def release(self, i, pairs):
        self.waits[i % self.n] = [(s.h if isinstance(s, Sem) else s, c) for s, c in pairs]


def wait_pairs(e, pairs):
    for h, c in pairs:
        if c > 0:
            e.wait_ge(h.h if isinstance(h, Sem) else h, c)


def _norm(pairs):
    return [(s.h if isinstance(s, Sem) else s, c) for (s, c) in pairs]


class DmaLanes:

    def __init__(self, nc, name, n):
        self.sems = [Sem(nc, f"{name}{i}") for i in range(n)]
        self.n = n
        self.i = 0

    def issue(self, k=16):
        lane = self.sems[self.i % self.n]
        self.i += 1
        prev = lane.n
        at, cnt = lane.inc(k)

        def emit(eng, instr_fn):
            if prev:
                eng.wait_ge(lane.h, prev)
            at(instr_fn())
        return emit, (lane, cnt)

    def finals(self):
        return [(s, s.n) for s in self.sems if s.n > 0]


class Ctx:
    def __init__(self, nc, T):
        self.nc = nc
        self.T = T
        for nm in ("ldh", "ldg", "sth", "stg", "mm", "vv", "ss", "pp", "qk", "ex", "av"):
            setattr(self, nm, Sem(nc, nm))
        self.Ls = DmaLanes(nc, "Ls", 12)
        self.Lg = DmaLanes(nc, "Lg", 8)
        self.x_bf = [nc.alloc_sbuf_tensor(f"x_bf{i}", [128, T], BF16) for i in range(8)]
        self.allones_bf = nc.alloc_sbuf_tensor("allones_bf", [128, 128], BF16)
        self.ones_f32 = nc.alloc_sbuf_tensor("ones_f32", [1, 128], F32)
        self.ones65 = nc.alloc_sbuf_tensor("ones65", [65, 64], F32)
        self.bias_att = nc.alloc_sbuf_tensor("bias_att", [128, 1], F32)
        self.zero_b = nc.alloc_sbuf_tensor("zero_b", [128, 1], F32)
        self.eps_b = nc.alloc_sbuf_tensor("eps_b", [128, 1], F32)
        self.ident = nc.alloc_sbuf_tensor("ident_sb", [128, 128], F32)
        self.pball = nc.alloc_psum_tensor("pball", [128, 4096], F32)
        self.ln_y = [nc.alloc_sbuf_tensor(f"ln_y{i}", [128, 512], F32) for i in range(8)]
        self.ln_br = [nc.alloc_sbuf_tensor(f"ln_br{i}", [128, 512], F32) for i in range(2)]
        self.ln_ybf = [nc.alloc_sbuf_tensor(f"ln_ybf{i}", [128, 512], BF16) for i in range(2)]
        self.ln_ysq = [nc.alloc_sbuf_tensor(f"ln_ysq{i}", [128, 512], BF16) for i in range(2)]
        self.ln_mt = nc.alloc_sbuf_tensor("ln_mt", [128, 512], F32)
        self.ln_e2 = nc.alloc_sbuf_tensor("ln_e2", [128, 512], F32)
        self.ln_rstd = nc.alloc_sbuf_tensor("ln_rstd", [128, 512], F32)
        self.ln_bln = nc.alloc_sbuf_tensor("ln_bln", [128, 512], F32)
        self.ln_xo = [nc.alloc_sbuf_tensor(f"ln_xo{i}", [128, 512], F32) for i in range(2)]
        self.ln_war_y = [[] for _ in range(8)]
        self.ln_sl_br = Slot(2)
        self.ln_sl_yb = Slot(2)
        self.ln_sl_xo = Slot(2)
        self.ln_it = 0
        self.ln_sout = {}
        self.ln_psum_war = []

    _uid = 0

    def uniq(self):
        Ctx._uid += 1
        return f"u{Ctx._uid}_"

    def bank(self, k, p0=0, p1=128, w=512):
        return self.pball[p0:p1, k * 512: k * 512 + w]

    def init_consts(self, prog, ident_dram):
        at, self.consts_v = self.vv.inc()

        def fv(v, at=at):
            v.memset(self.allones_bf[:], 1.0)
            v.memset(self.ones_f32[:], 1.0)
            v.memset(self.ones65[:], 1.0)
            v.memset(self.bias_att[:], -ATT_B)
            v.memset(self.zero_b[:], 0.0)
            at(v.memset(self.eps_b[:], EPS_LN))
        prog.emit("vector", fv)

        em, wp = self.Ls.issue()

        def fs(s, em=em):
            em(s, lambda: s.dma_start(self.ident[:], ident_dram))
        prog.emit("sync", fs)
        self.consts_l = wp

    def finals(self):
        return (self.Ls.finals() + self.Lg.finals() +
                [(s, s.n) for s in (self.mm, self.vv, self.ss, self.pp,
                                    self.qk, self.ex, self.av) if s.n > 0])

    def barrier(self, prog, pairs, engines=ENGINES):
        pairs = _norm(pairs)
        for eng in engines:
            def f(e, pairs=list(pairs)):
                wait_pairs(e, pairs)
            prog.emit(eng, f)



def load_w(ctx, prog, dram_ap, sb, rows, cols, wait=()):
    nci = rows // 128
    out_ap = sb[:].rearrange("p (ci d) -> p ci d", ci=nci)
    in_ap = dram_ap.rearrange("(ci p) d -> p ci d", p=128)
    em, wp = ctx.Lg.issue()

    def f(g, wait=_norm(wait), em=em):
        wait_pairs(g, wait)
        em(g, lambda: g.dma_start(out_ap, in_ap))
    prog.emit("gpsimd", f)
    return wp


def wt(sb, ci, dj, cols):
    return sb[:, ci * cols + dj * 128: ci * cols + dj * 128 + 128]



def emit_entry(ctx, prog, S_in):
    wps = []
    for ci in range(8):
        em, wp = ctx.Lg.issue()

        def f(g, ci=ci, em=em):
            em(g, lambda: g.dma_start(ctx.x_bf[ci][:], S_in[ci * 128:(ci + 1) * 128, :]))
        prog.emit("gpsimd", f)
        wps.append(wp)
    return wps



LN_MODE = "full"


class ResLN:

    def __init__(self, ctx, prog, S_in, Br, S_out, res_scale, next_sbuf, ready):
        self.ctx, self.prog = ctx, prog
        self.S_in, self.Br, self.S_out = S_in, Br, S_out
        self.res_scale = res_scale
        self.next_sbuf = next_sbuf
        self.nq = ctx.T // 512
        self.emitted = set()
        self.xbf_tq = [None] * self.nq
        self.sout_tq = [None] * self.nq
        self.sout_new = {}
        self.sout_prev = ctx.ln_sout
        self.psum47 = list(ready.get("psum47", []))
        self.psum03 = []
        self.xbf_free = []
        self.br_default = None
        self._first = True

    def emit_tq(self, tq, br_w=None):
        if tq in self.emitted:
            return
        self.emitted.add(tq)
        ctx, prog = self.ctx, self.prog
        TQ = 512
        S_in, Br, S_out = self.S_in, self.Br, self.S_out
        res_scale = self.res_scale
        passthru = (LN_MODE == "passthru")
        y_sb, br_ld = ctx.ln_y, ctx.ln_br
        ybf, ysq = ctx.ln_ybf, ctx.ln_ysq
        mt, e2, rstd, bln, xo = ctx.ln_mt, ctx.ln_e2, ctx.ln_rstd, ctx.ln_bln, ctx.ln_xo
        sl_br, sl_yb, sl_xo = ctx.ln_sl_br, ctx.ln_sl_yb, ctx.ln_sl_xo
        war_y = ctx.ln_war_y
        first = self._first
        self._first = False
        p47 = _norm(list(ctx.ln_psum_war) + list(self.psum47)) if first else []

        yready = [0] * 8
        for ci in range(8):
            it = ctx.ln_it
            ctx.ln_it += 1
            bsl, bwait = sl_br.acquire(it)
            em1, wp1 = ctx.Ls.issue()
            em2, wp2 = ctx.Ls.issue()

            def fl(s, ci=ci, tq=tq, bsl=bsl, bwait=bwait, wy=list(war_y[ci]),
                   sp=_norm(self.sout_prev.get((ci, tq), [])),
                   bw=_norm(br_w(ci) if br_w else []), em1=em1, em2=em2):
                wait_pairs(s, wy)
                wait_pairs(s, sp)
                em1(s, lambda: s.dma_start(y_sb[ci][:],
                                           S_in[ci * 128:(ci + 1) * 128, tq * TQ:(tq + 1) * TQ]))
                wait_pairs(s, bwait)
                wait_pairs(s, bw)
                em2(s, lambda: s.dma_start(br_ld[bsl][:],
                                           Br[ci * 128:(ci + 1) * 128, tq * TQ:(tq + 1) * TQ]))
            prog.emit("sync", fl)

            at, yc = ctx.vv.inc()

            def fv(v, ci=ci, lwp=_norm([wp1, wp2]), bsl=bsl, at=at):
                wait_pairs(v, lwp)
                if res_scale == 1.0:
                    at(v.tensor_tensor(y_sb[ci][:], y_sb[ci][:], br_ld[bsl][:], ALU.add))
                else:
                    at(v.scalar_tensor_tensor(y_sb[ci][:], y_sb[ci][:], float(res_scale),
                                              br_ld[bsl][:], ALU.mult, ALU.add))
            prog.emit("vector", fv)
            yready[ci] = yc
            sl_br.release(it, [(ctx.vv, yc)])

            ysl, ywait = sl_yb.acquire(it)
            if not passthru:
                atc, _ = ctx.ss.inc()
                at, sqc = ctx.ss.inc()

                def fs(s, ci=ci, yc=yc, ysl=ysl, ywait=ywait, at=at, atc=atc):
                    s.wait_ge(ctx.vv.h, yc)
                    wait_pairs(s, ywait)
                    atc(s.copy(ybf[ysl][:], y_sb[ci][:]))
                    at(s.square(ysq[ysl][:], y_sb[ci][:]))
                prog.emit("scalar", fs)

                at, mmc = ctx.mm.inc()

                def ft(t, ci=ci, sqc=sqc, ysl=ysl,
                       pw=(p47 if ci == 0 else []),
                       cw=(ctx.consts_v if first and ci == 0 else 0), at=at):
                    t.wait_ge(ctx.ss.h, sqc)
                    if cw:
                        t.wait_ge(ctx.vv.h, cw)
                    wait_pairs(t, pw)
                    t.matmul(ctx.bank(4), ctx.allones_bf[:], ybf[ysl][:],
                             start=(ci == 0), stop=(ci == 7), skip_group_check=True)
                    last = t.matmul(ctx.bank(6), ctx.allones_bf[:], ysq[ysl][:],
                                    start=(ci == 0), stop=(ci == 7), skip_group_check=True)
                    at(last)
                prog.emit("tensor", ft)
                sl_yb.release(it, [(ctx.mm, mmc)])
        stats_c = ctx.mm.n
        if passthru:
            at, ab_c = ctx.vv.inc()

            def fpt(v, at=at):
                v.memset(rstd[:], 1.0)
                at(v.memset(bln[:], 0.0))
            prog.emit("vector", fpt)
        else:
            fv1 = ctx.vv.inc()
            fv2 = ctx.vv.inc()
            fv3 = ctx.vv.inc()
            at, var_c = ctx.vv.inc()

            def fsv(v, c=stats_c, at=at, fv1=fv1, fv2=fv2, fv3=fv3):
                v.wait_ge(ctx.mm.h, c)
                fv1[0](v.tensor_scalar(mt[:], ctx.pball[:, 2048:2048 + TQ], 1.0 / D,
                                       None, ALU.mult))
                fv2[0](v.tensor_scalar(e2[:], ctx.pball[:, 3072:3072 + TQ], 1.0 / D,
                                       None, ALU.mult))
                v.wait_ge(ctx.vv.h, fv2[1])
                fv3[0](v.tensor_tensor(bln[:], mt[:], mt[:], ALU.mult))
                v.wait_ge(ctx.vv.h, fv3[1])
                at(v.tensor_tensor(e2[:], e2[:], bln[:], ALU.subtract))
            prog.emit("vector", fsv)

            fsl_ = ctx.ss.inc()
            at, rstd_c = ctx.ss.inc()

            def fss(s, c=var_c, at=at, fsl_=fsl_):
                s.wait_ge(ctx.vv.h, c)
                fsl_[0](s.activation(rstd[:], e2[:], AF.Ln, bias=ctx.eps_b[:, 0:1],
                                     scale=1.0))
                s.wait_ge(ctx.ss.h, fsl_[1])
                at(s.activation(rstd[:], rstd[:], AF.Exp, bias=ctx.zero_b[:, 0:1],
                                scale=-0.5))
            prog.emit("scalar", fss)

            at, ab_c = ctx.vv.inc()

            def fbv(v, c=rstd_c, at=at):
                v.wait_ge(ctx.ss.h, c)
                at(v.scalar_tensor_tensor(bln[:], mt[:], -1.0, rstd[:], ALU.mult, ALU.mult))
            prog.emit("vector", fbv)

        for ci in range(8):
            it = ctx.ln_it
            ctx.ln_it += 1
            xsl, xwait = sl_xo.acquire(it)
            atf = ctx.vv.inc()
            at, xoc = ctx.vv.inc()

            def fa(v, ci=ci, c=ab_c, xsl=xsl, xwait=xwait, at=at, atf=atf):
                v.wait_ge(ctx.vv.h, c)
                wait_pairs(v, xwait)
                atf[0](v.tensor_tensor(xo[xsl][:], y_sb[ci][:], rstd[:], ALU.mult))
                v.wait_ge(ctx.vv.h, atf[1])
                at(v.tensor_tensor(xo[xsl][:], xo[xsl][:], bln[:], ALU.add))
            prog.emit("vector", fa)
            war_y[ci] = [(ctx.vv, xoc)]

            ems, wps = ctx.Ls.issue()

            def fst(s, ci=ci, tq=tq, c=xoc, xsl=xsl, ems=ems):
                s.wait_ge(ctx.vv.h, c)
                ems(s, lambda: s.dma_start(
                    S_out[ci * 128:(ci + 1) * 128, tq * TQ:(tq + 1) * TQ], xo[xsl][:]))
            prog.emit("sync", fst)
            self.sout_new[(ci, tq)] = [wps]
            consumers = [wps]

            if self.next_sbuf:
                at, xbc = ctx.pp.inc()

                def fc(g, ci=ci, tq=tq, c=xoc, xsl=xsl,
                       xf=_norm(self.xbf_free) if ci == 0 else [], at=at):
                    g.wait_ge(ctx.vv.h, c)
                    wait_pairs(g, xf)
                    at(g.tensor_copy(ctx.x_bf[ci][:, tq * TQ:(tq + 1) * TQ], xo[xsl][:]))
                prog.emit("gpsimd", fc)
                consumers.append((ctx.pp, xbc))
            sl_xo.release(it, consumers)
        self.xbf_tq[tq] = [(ctx.pp, ctx.pp.n)]
        self.sout_tq[tq] = ctx.Ls.finals()

    def finish(self):
        ctx = self.ctx
        for tq in range(self.nq):
            bd = self.br_default
            self.emit_tq(tq, br_w=((lambda ci, tq=tq: bd(ci, tq)) if bd else None))
        ctx.ln_sout = self.sout_new
        ctx.ln_psum_war = [(ctx.vv, ctx.vv.n)]
        done = ctx.Ls.finals() + [(ctx.vv, ctx.vv.n), (ctx.mm, ctx.mm.n),
                                  (ctx.ss, ctx.ss.n)]
        if self.next_sbuf:
            done.append((ctx.pp, ctx.pp.n))
        return {"all": done, "xbf": self.xbf_tq, "sout": self.sout_tq,
                "br_free": [(ctx.vv, ctx.vv.n)],
                "psum": list(self.psum03),
                "psum47": [(ctx.vv, ctx.vv.n)]}



def alloc_attn_pre(ctx, stW, prog, Wq_d, Wk_d, Wv_d, wfree):
    u = ctx.uniq()
    nc, T = ctx.nc, ctx.T
    ntt = T // 128
    pre = {}
    pre["wq"] = stW.enter_context(nc.sbuf_tensor(u + "wq", [128, 8192], BF16))
    pre["wk"] = stW.enter_context(nc.sbuf_tensor(u + "wk", [128, 8192], BF16))
    pre["wv"] = stW.enter_context(nc.sbuf_tensor(u + "wv", [128, 8192], BF16))
    load_w(ctx, prog, Wq_d, pre["wq"], D, D, wait=wfree)
    load_w(ctx, prog, Wk_d, pre["wk"], D, D, wait=wfree)
    pre["lw"] = load_w(ctx, prog, Wv_d, pre["wv"], D, D, wait=wfree)
    return pre


def emit_attn(ctx, st, prog, Wo_d, Br, O_spill, gelu, ready, pre, lnr):
    u = ctx.uniq()
    nc, T = ctx.nc, ctx.T
    nqc = T // 512
    ntt = T // 128
    nkt = ntt

    import contextlib as _cl
    stA = st.enter_context(_cl.ExitStack())
    wq_sb, wk_sb, wv_sb = pre["wq"], pre["wk"], pre["wv"]
    lw = pre["lw"]
    P_T = stA.enter_context(nc.sbuf_tensor(u + "PT", [128, 2048], BF16))
    Q_grp = stA.enter_context(nc.sbuf_tensor(u + "Qg", [128, T], BF16))
    K_bufs = [stA.enter_context(nc.sbuf_tensor(u + f"Kg{i}", [128, T], BF16))
              for i in range(2)]
    V_bufs = [stA.enter_context(nc.sbuf_tensor(u + f"Vg{i}", [128, ntt * 130], BF16))
              for i in range(2)]
    o_raw = [stA.enter_context(nc.sbuf_tensor(u + f"oraw{i}", [128, 512], BF16))
             for i in range(4)]
    O_sl = [stA.enter_context(nc.sbuf_tensor(u + f"Og{i}", [128, 512], BF16))
            for i in range(3)]
    dn_sb = stA.enter_context(nc.sbuf_tensor(u + "dn", [65, 1024], F32))
    xbf = ready["xbf"]
    psum_free = _norm(ready.get("psum", []))
    br_free = _norm(ready.get("br_free", []))
    psum47_free = psum_free + _norm(ready.get("psum47", []))
    ready_all = _norm(ready.get("all", []))

    sl_dn = Slot(2)
    sl_or = Slot(4)
    sl_og = Slot(3)
    or_it = [0]
    og_it = [0]
    ogc_wps = []
    score_war = [list(psum_free), list(psum_free)]
    pt_war = [[], []]
    war_ob = list(psum47_free)
    war_rb = list(psum47_free)

    at, vones_c = ctx.vv.inc()

    def fones(v, at=at, ra=list(ready_all)):
        wait_pairs(v, ra)
        v.memset(V_bufs[0][:].rearrange("p (tt g c) -> p tt g c", g=2, c=65)
                 [:, :, :, 64:65], 1.0)
        at(v.memset(V_bufs[1][:].rearrange("p (tt g c) -> p tt g c", g=2, c=65)
                    [:, :, :, 64:65], 1.0))
    prog.emit("vector", fones)

    pj_war = {7: list(psum47_free) + [(ctx.vv, vones_c)],
              0: list(psum_free), 1: list(psum_free), 2: list(psum_free)}
    pj_gi = [0]
    task_q = []

    def queue_proj(thp, banks=(7,)):
        KD, VD = K_bufs[thp % 2], V_bufs[thp % 2]
        banks = list(banks)

        def k_or_q(tc, ci, wsb, DST, bk):
            at_, mmc = ctx.mm.inc()
            b7w = list(pj_war[bk]) if ci == 0 else []

            def fmm(t, b7w=_norm(b7w), tc=tc, ci=ci, thp=thp, wsb=wsb, bk=bk,
                    lww=(_norm([lw]) if thp == 0 and tc == 0 and ci == 0 else []),
                    xw=(_norm(xbf[tc]) if thp == 0 else []), at_=at_):
                wait_pairs(t, lww)
                wait_pairs(t, xw)
                wait_pairs(t, b7w)
                at_(t.matmul(ctx.bank(bk),
                             wsb[:, ci * 1024 + thp * 128: ci * 1024 + thp * 128 + 128],
                             ctx.x_bf[ci][:, tc * 512:(tc + 1) * 512],
                             start=(ci == 0), stop=(ci == 7), skip_group_check=True))
            prog.emit("tensor", fmm)
            if ci == 7:
                at2, evc = ctx.vv.inc()

                def fev(v, c=mmc, tc=tc, DST=DST, bk=bk, at2=at2):
                    v.wait_ge(ctx.mm.h, c)
                    at2(v.tensor_copy(DST[:, tc * 512:(tc + 1) * 512], ctx.bank(bk)))
                prog.emit("vector", fev)
                pj_war[bk] = [(ctx.vv, evc)]

        def v_task(tt, ci, bk):
            at_, mmc = ctx.mm.inc()
            b7w = list(pj_war[bk]) if ci == 0 else []

            def fmm(t, b7w=_norm(b7w), tt=tt, ci=ci, thp=thp, bk=bk,
                    xw=(_norm(xbf[tt // 4]) if thp == 0 else []), at_=at_):
                wait_pairs(t, xw)
                wait_pairs(t, b7w)
                at_(t.matmul(ctx.pball[:, bk * 512: bk * 512 + 128],
                             ctx.x_bf[ci][:, tt * 128:(tt + 1) * 128],
                             wv_sb[:, ci * 1024 + thp * 128: ci * 1024 + thp * 128 + 128],
                             start=(ci == 0), stop=(ci == 7), skip_group_check=True))
            prog.emit("tensor", fmm)
            if ci == 7:
                at2, evc = ctx.vv.inc()

                def fev(v, c=mmc, tt=tt, bk=bk, VD=VD, at2=at2):
                    v.wait_ge(ctx.mm.h, c)
                    src = ctx.pball[:, bk * 512: bk * 512 + 128]
                    src = src.rearrange("p (g c) -> p g c", g=2)
                    dst = VD[:, tt * 130: tt * 130 + 130].rearrange(
                        "p (g c) -> p g c", c=65)[:, :, 0:64]
                    at2(v.tensor_copy(dst, src))
                prog.emit("vector", fev)
                pj_war[bk] = [(ctx.vv, evc)]

        def nb():
            bk = banks[pj_gi[0] % len(banks)]
            pj_gi[0] += 1
            return bk

        for tc in range(nqc):
            bk = nb()
            for ci in range(8):
                task_q.append((2, (lambda tc=tc, ci=ci, bk=bk:
                                   k_or_q(tc, ci, wk_sb, KD, bk)), -1))
        for tt in range(ntt):
            bk = nb()
            for ci in range(8):
                task_q.append((1, (lambda tt=tt, ci=ci, bk=bk:
                                   v_task(tt, ci, bk)), -1))
        for tc in range(nqc):
            bk = nb()
            for ci in range(8):
                task_q.append((2, (lambda tc=tc, ci=ci, bk=bk:
                                   k_or_q(tc, ci, wq_sb, Q_grp, bk)), tc))

    def pop_tasks(cur_qc, budget):
        used = 0
        while task_q and used < budget:
            cost, fn, gate = task_q[0]
            if gate >= 0 and cur_qc <= gate:
                break
            task_q.pop(0)
            fn()
            used += cost

    def flush_tasks():
        while task_q:
            task_q.pop(0)[1]()

    for hp in range(8):
        K_grp, V_grp = K_bufs[hp % 2], V_bufs[hp % 2]
        if hp == 0:
            queue_proj(0, banks=(0, 1, 2, 7))
            flush_tasks()
        if hp + 1 < 8:
            queue_proj(hp + 1)
        proj_ready = (ctx.vv, ctx.vv.n)
        score_war[0] = score_war[0] + [proj_ready]
        score_war[1] = score_war[1] + [proj_ready]
        if hp == 7:
            lnr.xbf_free = [(ctx.mm, ctx.mm.n)]

        norm_last = 0
        pend = None

        def flush_pend(p):
            nonlocal war_rb, norm_last
            pqc, pdsl, prcc, por_i = p
            dr = 64 * pdsl
            rbw = list(war_rb)
            at, rbc = ctx.mm.inc()

            def frb(t, c=prcc, rbw=rbw, dr=dr, at=at):
                t.wait_ge(ctx.ss.h, c)
                wait_pairs(t, rbw)
                t.matmul(ctx.pball[0:64, 3072:3584], ctx.ones65[dr:dr + 1, 0:64],
                         dn_sb[dr:dr + 1, 0:512], start=True, stop=True,
                         skip_group_check=True)
                at(t.matmul(ctx.pball[64:128, 3072:3584], ctx.ones65[dr:dr + 1, 0:64],
                            dn_sb[dr:dr + 1, 512:1024], start=True, stop=True,
                            skip_group_check=True))
            prog.emit("tensor", frb)
            sl_dn.release(pdsl, [(ctx.mm, rbc)])

            og_i = og_it[0]
            og_it[0] += 1
            ogsl, ogwait = sl_og.acquire(og_i)
            posl = por_i % 4
            at, nmc = ctx.vv.inc()

            def fnm(v, c=rbc, posl=posl, ogsl=ogsl, ogwait=_norm(ogwait), at=at):
                v.wait_ge(ctx.mm.h, c)
                wait_pairs(v, ogwait)
                srcA = o_raw[posl][0:64, :]
                srcB = o_raw[posl][64:128, :]
                if gelu:
                    v.tensor_tensor(srcA, srcA, ctx.pball[0:64, 3072:3584], ALU.mult)
                    at(v.tensor_tensor(srcB, srcB, ctx.pball[64:128, 3072:3584],
                                       ALU.mult))
                else:
                    v.tensor_tensor(O_sl[ogsl][0:64, :], srcA,
                                    ctx.pball[0:64, 3072:3584], ALU.mult)
                    at(v.tensor_tensor(O_sl[ogsl][64:128, :], srcB,
                                       ctx.pball[64:128, 3072:3584], ALU.mult))
            prog.emit("vector", fnm)
            norm_last = nmc
            war_rb = [(ctx.vv, nmc)]
            if gelu:
                at, gc = ctx.ss.inc()

                def fg(s, c=nmc, posl=posl, ogsl=ogsl, at=at):
                    s.wait_ge(ctx.vv.h, c)
                    at(s.activation(O_sl[ogsl][:], o_raw[posl][:], AF.Gelu,
                                    bias=ctx.zero_b[:, 0:1], scale=1.0))
                prog.emit("scalar", fg)
                ogr = (ctx.ss, gc)
            else:
                ogr = (ctx.vv, nmc)
            sl_or.release(por_i, [ogr])

            emo, ogwp = ctx.Ls.issue()

            def fsp(s, hp=hp, pqc=pqc, ogsl=ogsl, ogr=_norm([ogr]), emo=emo):
                wait_pairs(s, ogr)
                emo(s, lambda: s.dma_start(
                    O_spill[hp * 128:(hp + 1) * 128, pqc * 512:(pqc + 1) * 512],
                    O_sl[ogsl][:]))
            prog.emit("sync", fsp)
            ogc_wps.append(ogwp)
            sl_og.release(og_i, [ogwp])

        for qc in range(nqc):
            ob_wait = list(war_ob)
            qk_counts = {}

            def emit_qk(kt, first):
                g = kt % 2
                sg_wait = list(score_war[g])
                at, qkc = ctx.qk.inc()
                qk_counts[kt] = qkc

                def fqk(t, g=g, kt=kt, qc=qc, sg_wait=sg_wait, K_grp=K_grp,
                        ob_wait=(ob_wait if first else []), at=at):
                    wait_pairs(t, sg_wait)
                    wait_pairs(t, ob_wait)
                    t.matmul(ctx.pball[:, g * 1024: g * 1024 + 512],
                             K_grp[0:64, kt * 128:(kt + 1) * 128],
                             Q_grp[0:64, qc * 512:(qc + 1) * 512],
                             start=True, stop=True)
                    at(t.matmul(ctx.pball[:, g * 1024 + 512: g * 1024 + 1024],
                                K_grp[64:128, kt * 128:(kt + 1) * 128],
                                Q_grp[64:128, qc * 512:(qc + 1) * 512],
                                start=True, stop=True))
                prog.emit("tensor", fqk)

            emit_qk(0, True)
            emit_qk(1, False)
            for kt in range(nkt):
                g = kt % 2
                pt_wait = list(pt_war[g])
                at, exc = ctx.ex.inc()

                def fex(s, g=g, c=qk_counts[kt], pt_wait=pt_wait,
                        law=(ready_all if hp == 0 and qc == 0 and kt == 0 else []),
                        at=at):
                    s.wait_ge(ctx.qk.h, c)
                    wait_pairs(s, law)
                    wait_pairs(s, pt_wait)
                    at(s.activation(P_T[:, g * 1024:(g + 1) * 1024],
                                    ctx.pball[:, g * 1024:(g + 1) * 1024],
                                    AF.Exp, bias=ctx.bias_att[:, 0:1], scale=0.125))
                prog.emit("scalar", fex)
                score_war[g] = [(ctx.ex, exc)]

                at, avc = ctx.av.inc()

                def fav(t, g=g, kt=kt, c=exc, V_grp=V_grp, at=at):
                    t.wait_ge(ctx.ex.h, c)
                    t.matmul(ctx.bank(4, p1=65), V_grp[:, kt * 130: kt * 130 + 65],
                             P_T[:, g * 1024: g * 1024 + 512],
                             start=(kt == 0), stop=(kt == nkt - 1))
                    at(t.matmul(ctx.bank(5, p1=65), V_grp[:, kt * 130 + 65: kt * 130 + 130],
                                P_T[:, g * 1024 + 512: g * 1024 + 1024],
                                start=(kt == 0), stop=(kt == nkt - 1)))
                prog.emit("tensor", fav)
                pt_war[g] = [(ctx.av, avc)]

                if kt + 2 < nkt:
                    emit_qk(kt + 2, False)

                if kt == 2 and pend is not None:
                    flush_pend(pend)
                    pend = None

                pop_tasks(qc, 2)
            av_all = ctx.av.n

            or_i = or_it[0]
            or_it[0] += 1
            osl, owait = sl_or.acquire(or_i)
            dsl, dwait = sl_dn.acquire(qc)
            dr = 64 * dsl
            atd = ctx.vv.inc()
            at, evc = ctx.vv.inc()

            def fdn(v, c=av_all, dr=dr, dwait=_norm(dwait), owait=_norm(owait),
                    osl=osl, at=at, atd=atd):
                v.wait_ge(ctx.av.h, c)
                wait_pairs(v, dwait)
                wait_pairs(v, owait)
                v.tensor_copy(dn_sb[dr:dr + 1, 0:512], ctx.bank(4, p0=64, p1=65))
                atd[0](v.tensor_copy(dn_sb[dr:dr + 1, 512:1024],
                                     ctx.bank(5, p0=64, p1=65)))
                v.tensor_copy(o_raw[osl][0:64, :], ctx.bank(4, p1=64))
                at(v.tensor_copy(o_raw[osl][64:128, :], ctx.bank(5, p1=64)))
            prog.emit("vector", fdn)
            war_ob = [(ctx.vv, evc)]

            atl = ctx.ss.inc()
            at, rcc = ctx.ss.inc()

            def frc(s, dr=dr, c=evc, at=at, atl=atl):
                s.wait_ge(ctx.vv.h, c)
                atl[0](s.activation(dn_sb[dr:dr + 1, 0:1024], dn_sb[dr:dr + 1, 0:1024],
                                    AF.Ln, bias=ctx.zero_b[0:1, 0:1], scale=1.0))
                s.wait_ge(ctx.ss.h, atl[1])
                at(s.activation(dn_sb[dr:dr + 1, 0:1024], dn_sb[dr:dr + 1, 0:1024],
                                AF.Exp, bias=ctx.zero_b[0:1, 0:1], scale=-1.0))
            prog.emit("scalar", frc)
            pend = (qc, dsl, rcc, or_i)

        flush_tasks()
        if pend is not None:
            flush_pend(pend)
            pend = None

    lnr.psum47 = [(ctx.vv, ctx.vv.n), (ctx.ss, ctx.ss.n)]
    stA.close()
    wo_sb = wq_sb
    lwo = load_w(ctx, prog, Wo_d, wo_sb, D, D, wait=[(ctx.qk, ctx.qk.n)])
    spill_all = ctx.Ls.finals()
    sl_pj = Slot(3)
    pjit = 0
    for i in range(3):
        sl_pj.waits[i] = [(ctx.ex, ctx.ex.n)] + list(psum_free)

    nth = max(1, T // 1024)
    THW = min(1024, T)
    O_ld = [st.enter_context(nc.sbuf_tensor(u + f"old{i}", [128, THW], BF16)) for i in range(8)]
    brst = [st.enter_context(nc.sbuf_tensor(u + f"brst{i}", [128, THW], F32)) for i in range(2)]

    sl_br = Slot(2)
    war_old = []
    pend_ln = None
    ntq_th = THW // 512
    for th in range(nth):
        ol_wps = []
        for ci in range(8):
            em, wp = ctx.Ls.issue()
            ol_wps.append(wp)

            def fol(s, ci=ci, th=th, wo=list(war_old),
                    spill=_norm(spill_all) if th == 0 and ci == 0 else [], em=em):
                wait_pairs(s, spill)
                if ci == 0:
                    wait_pairs(s, wo)
                em(s, lambda: s.dma_start(
                    O_ld[ci][:],
                    O_spill[ci * 128:(ci + 1) * 128, th * THW:(th + 1) * THW]))
            prog.emit("sync", fol)

        if pend_ln is not None:
            pth, pwps = pend_ln
            for tq in range(pth * ntq_th, (pth + 1) * ntq_th):
                lnr.emit_tq(tq, br_w=lambda ci, pwps=pwps: [pwps[ci]])
            pend_ln = None

        wps_th = {}
        for dj in range(8):
            slot, swait = sl_br.acquire(th * 8 + dj)
            evc = 0
            for tc in range(THW // 512):
                b, bwait = sl_pj.acquire(pjit)
                at, mmc = ctx.mm.inc()

                def fmm(t, b=b, bwait=bwait, dj=dj, tc=tc,
                        olc=_norm(ol_wps), lwo=_norm([lwo]), at=at):
                    wait_pairs(t, olc)
                    wait_pairs(t, lwo)
                    wait_pairs(t, bwait)
                    last = None
                    for ci in range(8):
                        last = t.matmul(ctx.bank(b), wt(wo_sb, ci, dj, 1024),
                                        O_ld[ci][:, tc * 512:(tc + 1) * 512],
                                        start=(ci == 0), stop=(ci == 7))
                    at(last)
                prog.emit("tensor", fmm)

                at, evc = ctx.vv.inc()

                def fev(v, b=b, c=mmc, slot=slot, tc=tc,
                        swait=(swait if tc == 0 else []), at=at):
                    v.wait_ge(ctx.mm.h, c)
                    wait_pairs(v, swait)
                    at(v.tensor_copy(brst[slot][:, tc * 512:(tc + 1) * 512], ctx.bank(b)))
                prog.emit("vector", fev)
                sl_pj.release(pjit, [(ctx.vv, evc)])
                pjit += 1

            em, wp = ctx.Ls.issue()

            def fst(s, dj=dj, th=th, slot=slot, c=evc, em=em,
                    bf=(br_free if th == 0 and dj == 0 else [])):
                s.wait_ge(ctx.vv.h, c)
                wait_pairs(s, bf)
                em(s, lambda: s.dma_start(
                    Br[dj * 128:(dj + 1) * 128, th * THW:(th + 1) * THW], brst[slot][:]))
            prog.emit("sync", fst)
            wps_th[dj] = wp
            sl_br.release(th * 8 + dj, [wp])
        pend_ln = (th, wps_th)
        war_old = [(ctx.mm, ctx.mm.n)]

    if pend_ln is not None:
        pth, pwps = pend_ln
        for tq in range(pth * ntq_th, (pth + 1) * ntq_th):
            lnr.emit_tq(tq, br_w=lambda ci, pwps=pwps: [pwps[ci]])
    lnr.psum03 = [(ctx.mm, ctx.mm.n), (ctx.vv, ctx.vv.n)]
    return ctx.finals()



def alloc_ffn0_pre(ctx, stW, prog, Wr_d, Wk_d, Wv_d, mur_d, muk_d, wfree):
    u = ctx.uniq()
    nc = ctx.nc
    pre = {}
    pre["wr"] = stW.enter_context(nc.sbuf_tensor(u + "fwr", [128, 8192], BF16))
    pre["wk"] = stW.enter_context(nc.sbuf_tensor(u + "fwk", [128, 8192], BF16))
    pre["wv"] = stW.enter_context(nc.sbuf_tensor(u + "fwv", [128, 8192], BF16))
    pre["mur"] = stW.enter_context(nc.sbuf_tensor(u + "mur", [128, 8], F32))
    pre["muk"] = stW.enter_context(nc.sbuf_tensor(u + "muk", [128, 8], F32))
    load_w(ctx, prog, Wr_d, pre["wr"], D, D, wait=wfree)
    load_w(ctx, prog, Wk_d, pre["wk"], D, D, wait=wfree)
    pre["lw"] = load_w(ctx, prog, Wv_d, pre["wv"], D, D, wait=wfree)
    mu_wps = []
    for ci in range(8):
        em1, wp1 = ctx.Ls.issue()
        em2, wp2 = ctx.Ls.issue()
        mu_wps += [wp1, wp2]

        def fmu(s, ci=ci, em1=em1, em2=em2, wfree=_norm(wfree if ci == 0 else [])):
            wait_pairs(s, wfree)
            em1(s, lambda: s.dma_start(pre["mur"][:, ci:ci + 1],
                                       mur_d[ci * 128:(ci + 1) * 128, 0:1]))
            em2(s, lambda: s.dma_start(pre["muk"][:, ci:ci + 1],
                                       muk_d[ci * 128:(ci + 1) * 128, 0:1]))
        prog.emit("sync", fmu)
    pre["mu_wps"] = mu_wps
    return pre


def emit_ffn0(ctx, st, prog, Br, ready, pre, lnr):
    u = ctx.uniq()
    nc, T = ctx.nc, ctx.T
    TQ = min(512, T)
    nq = T // TQ
    nsub = TQ // 512

    wr_sb, wk_sb, wv_sb = pre["wr"], pre["wk"], pre["wv"]
    mur, muk = pre["mur"], pre["muk"]
    lw = pre["lw"]
    mu_wps = pre["mu_wps"]
    xbf = ready["xbf"]
    psum_free = _norm(ready.get("psum", []))
    br_free = _norm(ready.get("br_free", []))
    xr_bf = [st.enter_context(nc.sbuf_tensor(u + f"xr{i}", [128, TQ], BF16)) for i in range(8)]
    xk_bf = [st.enter_context(nc.sbuf_tensor(u + f"xk{i}", [128, TQ], BF16)) for i in range(8)]
    dtmp = st.enter_context(nc.sbuf_tensor(u + "dtmp", [128, TQ], BF16))
    r_grp = st.enter_context(nc.sbuf_tensor(u + "rgrp", [128, 8 * TQ], BF16))
    ka_grp = st.enter_context(nc.sbuf_tensor(u + "kagrp", [128, 8 * TQ], BF16))
    kmx = st.enter_context(nc.sbuf_tensor(u + "kmx", [128, 512], BF16))
    brst = [st.enter_context(nc.sbuf_tensor(u + f"fbr{i}", [128, TQ], F32)) for i in range(2)]

    sl_mm = Slot(3)
    for i in range(3):
        sl_mm.waits[i] = list(psum_free)
    sl_br = Slot(2)
    pjit = 0
    war_xr = []

    tpc = TQ // 512
    pend_ln = None
    for tq in range(nq):
        g0 = tq * TQ
        for ci in range(8):
            at, xrc = ctx.vv.inc()

            def fx(v, ci=ci, g0=g0, wx=list(war_xr),
                   mc=(_norm(mu_wps) if tq == 0 and ci == 0 else []),
                   xw=_norm(xbf[(g0 + TQ) // 512 - 1]),
                   at=at):
                wait_pairs(v, mc)
                wait_pairs(v, xw)
                wait_pairs(v, wx)
                x = ctx.x_bf[ci]
                if g0 == 0:
                    v.tensor_copy(dtmp[:, 0:1], x[:, 0:1])
                    v.tensor_tensor(dtmp[:, 1:TQ], x[:, 1:TQ], x[:, 0:TQ - 1], ALU.subtract)
                    v.scalar_tensor_tensor(xr_bf[ci][:, 1:TQ], dtmp[:, 1:TQ],
                                           mur[:, ci:ci + 1], x[:, 0:TQ - 1],
                                           ALU.mult, ALU.add)
                    v.tensor_scalar(xr_bf[ci][:, 0:1], x[:, 0:1], mur[:, ci:ci + 1],
                                    None, ALU.mult)
                    v.scalar_tensor_tensor(xk_bf[ci][:, 1:TQ], dtmp[:, 1:TQ],
                                           muk[:, ci:ci + 1], x[:, 0:TQ - 1],
                                           ALU.mult, ALU.add)
                    at(v.tensor_scalar(xk_bf[ci][:, 0:1], x[:, 0:1], muk[:, ci:ci + 1],
                                       None, ALU.mult))
                else:
                    v.tensor_tensor(dtmp[:], x[:, g0:g0 + TQ], x[:, g0 - 1:g0 + TQ - 1],
                                    ALU.subtract)
                    v.scalar_tensor_tensor(xr_bf[ci][:], dtmp[:], mur[:, ci:ci + 1],
                                           x[:, g0 - 1:g0 + TQ - 1], ALU.mult, ALU.add)
                    at(v.scalar_tensor_tensor(xk_bf[ci][:], dtmp[:], muk[:, ci:ci + 1],
                                              x[:, g0 - 1:g0 + TQ - 1], ALU.mult, ALU.add))
            prog.emit("vector", fx)
        xr_all = ctx.vv.n

        if pend_ln is not None:
            pq, pwps = pend_ln
            lnr.xbf_free = [(ctx.vv, xr_all)]
            for ltq in range(pq * tpc, (pq + 1) * tpc):
                lnr.emit_tq(ltq, br_w=lambda ci, pwps=pwps: [pwps[ci]])
            pend_ln = None

        r_last = 0
        ka_last = 0
        war_kmx = []
        for which in ("r", "k"):
            wsb = wr_sb if which == "r" else wk_sb
            src = xr_bf if which == "r" else xk_bf
            for dj in range(8):
                for tc in range(nsub):
                    b, bwait = sl_mm.acquire(pjit)
                    at, mmc = ctx.mm.inc()

                    def fmm(t, b=b, bwait=bwait, dj=dj, tc=tc, wsb=wsb, src=src,
                            xc=xr_all, lw=_norm([lw]), at=at):
                        t.wait_ge(ctx.vv.h, xc)
                        wait_pairs(t, lw)
                        wait_pairs(t, bwait)
                        last = None
                        for ci in range(8):
                            last = t.matmul(ctx.bank(b), wt(wsb, ci, dj, 1024),
                                            src[ci][:, tc * 512:(tc + 1) * 512],
                                            start=(ci == 0), stop=(ci == 7))
                        at(last)
                    prog.emit("tensor", fmm)

                    if which == "r":
                        at, r_last = ctx.ss.inc()

                        def fev(s, b=b, c=mmc, dj=dj, tc=tc, at=at):
                            s.wait_ge(ctx.mm.h, c)
                            at(s.activation(
                                r_grp[:, dj * TQ + tc * 512: dj * TQ + (tc + 1) * 512],
                                ctx.bank(b), AF.Sigmoid, bias=ctx.zero_b[:, 0:1], scale=1.0))
                        prog.emit("scalar", fev)
                        sl_mm.release(pjit, [(ctx.ss, r_last)])
                    else:
                        at, kmc = ctx.vv.inc()

                        def fev(v, b=b, c=mmc, wk=list(war_kmx), at=at):
                            v.wait_ge(ctx.mm.h, c)
                            wait_pairs(v, wk)
                            at(v.tensor_scalar(kmx[:], ctx.bank(b), 0.0, None, ALU.max))
                        prog.emit("vector", fev)
                        sl_mm.release(pjit, [(ctx.vv, kmc)])

                        at, ka_last = ctx.ss.inc()

                        def fsq(s, c=kmc, dj=dj, tc=tc, at=at):
                            s.wait_ge(ctx.vv.h, c)
                            at(s.square(
                                ka_grp[:, dj * TQ + tc * 512: dj * TQ + (tc + 1) * 512],
                                kmx[:]))
                        prog.emit("scalar", fsq)
                        war_kmx = [(ctx.ss, ka_last)]
                    pjit += 1

        wps_tq = {}
        for dj in range(8):
            slot, swait = sl_br.acquire(tq * 8 + dj)
            evl = 0
            for tc in range(nsub):
                b, bwait = sl_mm.acquire(pjit)
                at, mmc = ctx.mm.inc()

                def fmm(t, b=b, bwait=bwait, dj=dj, tc=tc, kc=ka_last, at=at):
                    t.wait_ge(ctx.ss.h, kc)
                    wait_pairs(t, bwait)
                    last = None
                    for fi in range(8):
                        last = t.matmul(ctx.bank(b), wt(wv_sb, fi, dj, 1024),
                                        ka_grp[:, fi * TQ + tc * 512: fi * TQ + (tc + 1) * 512],
                                        start=(fi == 0), stop=(fi == 7))
                    at(last)
                prog.emit("tensor", fmm)

                at, evl = ctx.vv.inc()

                def fev(v, b=b, c=mmc, dj=dj, tc=tc, slot=slot, rc=r_last,
                        swait=(swait if tc == 0 else []), at=at):
                    v.wait_ge(ctx.mm.h, c)
                    v.wait_ge(ctx.ss.h, rc)
                    wait_pairs(v, swait)
                    at(v.tensor_tensor(
                        brst[slot][:, tc * 512:(tc + 1) * 512],
                        r_grp[:, dj * TQ + tc * 512: dj * TQ + (tc + 1) * 512],
                        ctx.bank(b), ALU.mult))
                prog.emit("vector", fev)
                sl_mm.release(pjit, [(ctx.vv, evl)])
                pjit += 1

            em, wp = ctx.Ls.issue()

            def fst(s, dj=dj, tq=tq, slot=slot, c=evl, em=em,
                    bf=(br_free if tq == 0 and dj == 0 else [])):
                s.wait_ge(ctx.vv.h, c)
                wait_pairs(s, bf)
                em(s, lambda: s.dma_start(
                    Br[dj * 128:(dj + 1) * 128, tq * TQ:(tq + 1) * TQ], brst[slot][:]))
            prog.emit("sync", fst)
            wps_tq[dj] = wp
            sl_br.release(tq * 8 + dj, [wp])
        pend_ln = (tq, wps_tq)
        war_xr = [(ctx.mm, ctx.mm.n)]

    if pend_ln is not None:
        pq, pwps = pend_ln
        lnr.xbf_free = [(ctx.vv, ctx.vv.n)]
        for ltq in range(pq * tpc, (pq + 1) * tpc):
            lnr.emit_tq(ltq, br_w=lambda ci, pwps=pwps: [pwps[ci]])
    lnr.psum03 = [(ctx.mm, ctx.mm.n), (ctx.vv, ctx.vv.n), (ctx.ss, ctx.ss.n)]
    return ctx.finals()



def alloc_moe_pre(ctx, stW, prog, Wg_d, Wu_d, Wd_d, Wgs_d, Wus_d, Wds_d, wfree):
    u = ctx.uniq()
    nc = ctx.nc
    pre = {}
    pre["wg"] = [stW.enter_context(nc.sbuf_tensor(u + f"wge{i}", [128, 6144], BF16))
                 for i in range(2)]
    pre["wu"] = [stW.enter_context(nc.sbuf_tensor(u + f"wue{i}", [128, 6144], BF16))
                 for i in range(2)]
    pre["wd"] = [stW.enter_context(nc.sbuf_tensor(u + f"wde{i}", [128, 6144], BF16))
                 for i in range(2)]
    order = [NE] + list(range(NE))
    pre["order"] = order
    pre["wslots"] = {}

    def emit_wload(eidx, wait):
        e = order[eidx]
        slot = eidx % 2
        if e == NE:
            g_d, u_d, d_d = Wgs_d, Wus_d, Wds_d
        else:
            g_d, u_d, d_d = Wg_d[e], Wu_d[e], Wd_d[e]
        w1 = load_w(ctx, prog, g_d, pre["wg"][slot], D, FF, wait=wait)
        w2 = load_w(ctx, prog, u_d, pre["wu"][slot], D, FF, wait=())
        w3 = load_w(ctx, prog, d_d, pre["wd"][slot], FF, D, wait=())
        pre["wslots"][eidx] = (slot, [w1, w2, w3])
    pre["emit_wload"] = emit_wload
    emit_wload(0, wfree)
    emit_wload(1, wfree)
    return pre


def emit_moe(ctx, st, prog, Rt_d, Bias_d, Wg_d, Wu_d, Wd_d, Wgs_d, Wus_d, Wds_d,
             Wrt_scratch, Br, S_in_d, ready, pre, lnr):
    u = ctx.uniq()
    nc, T = ctx.nc, ctx.T
    ntt = T // 128
    nq = T // 512
    xbf = ready["xbf"]
    psum_free = _norm(ready.get("psum", []))
    br_free = _norm(ready.get("br_free", []))
    sout = _norm(ready.get("sout", [[]])[-1])

    import contextlib as _cl
    wg_sb, wu_sb, wd_sb = pre["wg"], pre["wu"], pre["wd"]
    wrow = [st.enter_context(nc.sbuf_tensor(u + "wrow0", [1, 512], BF16))]
    sg_t = st.enter_context(nc.sbuf_tensor(u + "sgt", [128, 512], BF16))
    hp_sb = st.enter_context(nc.sbuf_tensor(u + "hps", [128, 3072], BF16))
    brmo = [st.enter_context(nc.sbuf_tensor(u + f"brmo{i}", [128, 2048], F32))
            for i in range(2)]

    order = pre["order"]
    wslots = pre["wslots"]
    emit_wload = pre["emit_wload"]

    sl_g = Slot(2)
    sl_u = Slot(2)
    sl_wb = Slot(1)
    sl_d = Slot(2)
    sl_bm = Slot(2)
    for s_ in (sl_g, sl_u, sl_d):
        for i in range(s_.n):
            s_.waits[i] = list(psum_free)
    psum47_free = psum_free + _norm(ready.get("psum47", []))
    sl_wb.waits[0] = list(psum47_free)
    for i in range(sl_d.n):
        sl_d.waits[i] = list(psum47_free)
    state = {"war_sg": [], "war_hp": [], "war_wrow": [], "wrt_wp": [],
             "war_wrow2": [[], []], "writ": 0,
             "git": 0, "uit": 0, "wbit": 0, "dit": 0}
    acc_last = [[] for _ in range(nq)]
    acc_prev = {(tq, h): [] for tq in range(nq) for h in range(2)}

    def expert_body(eidx):
        e = order[eidx]
        slot, wld = wslots[eidx]
        shared = (e == NE)


        for tq in range(nq):
            wb_c = 0
            if not shared:
                wsl = 0
                em, wrow_wp = ctx.Ls.issue()

                def fwr(s, e=e, tq=tq, wsl=wsl,
                        c=(_norm(state["wrt_wp"]) if tq == 0 else []),
                        ww=_norm(state["war_wrow2"][wsl]), em=em):
                    wait_pairs(s, c)
                    wait_pairs(s, ww)
                    em(s, lambda: s.dma_start(
                        wrow[wsl][:], Wrt_scratch[e:e + 1, tq * 512:(tq + 1) * 512]))
                prog.emit("sync", fwr)

                b_wb, wbwait = sl_wb.acquire(state["wbit"])
                at, wb_c = ctx.mm.inc()

                def fwb(t, wsl=wsl, wbwait=wbwait, c=_norm([wrow_wp]), at=at):
                    wait_pairs(t, c)
                    wait_pairs(t, wbwait)
                    at(t.matmul(ctx.bank(4), ctx.allones_bf[0:1, :],
                                wrow[wsl][:], start=True, stop=True))
                prog.emit("tensor", fwb)
                state["war_wrow2"][wsl] = [(ctx.mm, wb_c)]
                state["writ"] += 1

            h_last = 0
            for fj in range(6):
                bg, gwait = sl_g.acquire(state["git"])
                at, g_c = ctx.mm.inc()

                def fg(t, bg=bg, gwait=gwait, fj=fj, tq=tq, slot=slot,
                       wld=_norm(wld), xw=(_norm(xbf[tq]) if fj == 0 else []), at=at):
                    wait_pairs(t, wld)
                    wait_pairs(t, xw)
                    wait_pairs(t, gwait)
                    last = None
                    for ci in range(8):
                        last = t.matmul(ctx.bank(bg), wt(wg_sb[slot], ci, fj, FF),
                                        ctx.x_bf[ci][:, tq * 512:(tq + 1) * 512],
                                        start=(ci == 0), stop=(ci == 7))
                    at(last)
                prog.emit("tensor", fg)

                at, sil_c = ctx.ss.inc()

                def fsl(s, bg=bg, c=g_c, wsg=list(state["war_sg"]), at=at):
                    s.wait_ge(ctx.mm.h, c)
                    wait_pairs(s, wsg)
                    at(s.activation(sg_t[:], ctx.bank(bg), AF.Silu,
                                    bias=ctx.zero_b[:, 0:1], scale=1.0))
                prog.emit("scalar", fsl)
                sl_g.release(state["git"], [(ctx.ss, sil_c)])
                state["git"] += 1

                bu, uwait = sl_u.acquire(state["uit"])
                at, u_c = ctx.mm.inc()

                def fu(t, bu=bu, uwait=uwait, fj=fj, tq=tq, slot=slot, at=at):
                    wait_pairs(t, uwait)
                    last = None
                    for ci in range(8):
                        last = t.matmul(ctx.bank(2 + bu), wt(wu_sb[slot], ci, fj, FF),
                                        ctx.x_bf[ci][:, tq * 512:(tq + 1) * 512],
                                        start=(ci == 0), stop=(ci == 7))
                    at(last)
                prog.emit("tensor", fu)

                atf = ctx.vv.inc() if not shared else None
                at, h_last = ctx.vv.inc()

                def fh(v, bu=bu, fj=fj, c1=sil_c, c2=u_c, shared=shared,
                       whp=list(state["war_hp"]), at=at, atf=atf):
                    v.wait_ge(ctx.ss.h, c1)
                    v.wait_ge(ctx.mm.h, c2)
                    wait_pairs(v, whp)
                    dst = hp_sb[:, fj * 512:(fj + 1) * 512]
                    if not shared:
                        atf[0](v.tensor_tensor(dst, sg_t[:], ctx.bank(2 + bu), ALU.mult))
                        v.wait_ge(ctx.vv.h, atf[1])
                        at(v.tensor_tensor(dst, dst, ctx.bank(4), ALU.mult))
                    else:
                        at(v.tensor_tensor(dst, sg_t[:], ctx.bank(2 + bu), ALU.mult))
                prog.emit("vector", fh)
                state["war_sg"] = [(ctx.vv, h_last)]
                sl_u.release(state["uit"], [(ctx.vv, h_last)])
                state["uit"] += 1
            if not shared:
                sl_wb.release(state["wbit"], [(ctx.vv, h_last)])
                state["wbit"] += 1

            ev_last = 0
            for half in range(2):
                bsl, bswait = sl_bm.acquire((eidx * nq + tq) * 2 + half)
                for dj4 in range(4):
                    dj = half * 4 + dj4
                    bd, dwait = sl_d.acquire(state["dit"])
                    at, d_c = ctx.mm.inc()

                    def fd(t, bd=bd, dwait=dwait, dj=dj, slot=slot, c=h_last, at=at):
                        t.wait_ge(ctx.vv.h, c)
                        wait_pairs(t, dwait)
                        last = None
                        for fi in range(6):
                            last = t.matmul(ctx.bank(5 + bd), wt(wd_sb[slot], fi, dj, D),
                                            hp_sb[:, fi * 512:(fi + 1) * 512],
                                            start=(fi == 0), stop=(fi == 5))
                        at(last)
                    prog.emit("tensor", fd)

                    at, ev_last = ctx.vv.inc()

                    def fde(v, bd=bd, dj4=dj4, c=d_c, bsl=bsl,
                            bswait=(bswait if dj4 == 0 else []), at=at):
                        v.wait_ge(ctx.mm.h, c)
                        wait_pairs(v, bswait)
                        at(v.tensor_copy(brmo[bsl][:, dj4 * 512:(dj4 + 1) * 512],
                                         ctx.bank(5 + bd)))
                    prog.emit("vector", fde)
                    sl_d.release(state["dit"], [(ctx.vv, ev_last)])
                    state["dit"] += 1

                out_ap = Br[half * 512:(half + 1) * 512,
                            tq * 512:(tq + 1) * 512].rearrange("(dj p) c -> p dj c", p=128)
                in_ap = brmo[bsl][:].rearrange("p (dj c) -> p dj c", dj=4)
                em, acc_wp = ctx.Lg.issue()

                def fac(g, c=ev_last, prev=_norm(acc_prev[(tq, half)]), shared=shared,
                        out_ap=out_ap, in_ap=in_ap, em=em,
                        bf=(br_free if shared else [])):
                    g.wait_ge(ctx.vv.h, c)
                    wait_pairs(g, bf)
                    wait_pairs(g, prev)
                    if shared:
                        em(g, lambda: g.dma_start(out_ap, in_ap))
                    else:
                        em(g, lambda: g.dma_start(out_ap, in_ap, accum_op=ALU.add))
                prog.emit("gpsimd", fac)
                acc_prev[(tq, half)] = [acc_wp]
                sl_bm.release((eidx * nq + tq) * 2 + half, [acc_wp])
            acc_last[tq] = acc_prev[(tq, 0)] + acc_prev[(tq, 1)]
            state["war_hp"] = [(ctx.mm, ctx.mm.n)]
            if not shared:
                state["war_wrow"] = [(ctx.mm, wb_c)]

        if eidx + 2 < len(order):
            emit_wload(eidx + 2, [(ctx.mm, ctx.mm.n)])

    expert_body(0)

    stR = st.enter_context(_cl.ExitStack())
    rt_sb = stR.enter_context(nc.sbuf_tensor(u + "rt", [128, 64], F32))
    xf32 = stR.enter_context(nc.sbuf_tensor(u + "xf", [128, min(1024, T)], F32))
    s_tm = stR.enter_context(nc.sbuf_tensor(u + "stm", [128, ntt * 8], F32))
    sp_tm = stR.enter_context(nc.sbuf_tensor(u + "sptm", [128, ntt * 8], F32))
    m1t = stR.enter_context(nc.sbuf_tensor(u + "m1t", [128, ntt], F32))
    m2t = stR.enter_context(nc.sbuf_tensor(u + "m2t", [128, ntt], F32))
    et = stR.enter_context(nc.sbuf_tensor(u + "et", [128, ntt * 8], F32))
    wmt = stR.enter_context(nc.sbuf_tensor(u + "wmt", [128, ntt * 8], F32))
    brow = stR.enter_context(nc.sbuf_tensor(u + "brow", [1, ntt * 8], F32))
    wfm = [stR.enter_context(nc.sbuf_tensor(u + f"wfm{i}", [8, 128], BF16))
           for i in range(2)]

    em, lrt_wp = ctx.Ls.issue()

    def flrt(s, em=em):
        em(s, lambda: s.dma_start(rt_sb[:].rearrange("p (ci d) -> p ci d", ci=8),
                                  Rt_d.rearrange("(ci p) d -> p ci d", p=128)))
    prog.emit("sync", flrt)
    _rt_mm_base = ctx.mm.n
    XW = min(1024, T)
    nxh = T // XW
    ngrp = 8 * nxh
    xl_wps = [None] * ngrp
    for gi in range(ngrp):
        ci, h2 = gi // nxh, gi % nxh
        em, xl_wps[gi] = ctx.Ls.issue()

        def fxl(s, ci=ci, h2=h2, gi=gi, em=em, so=(sout if gi == 0 else [])):
            wait_pairs(s, so)
            if gi >= 1:
                s.wait_ge(ctx.mm.h, _rt_mm_base + gi)
            em(s, lambda: s.dma_start(
                xf32[:], S_in_d[ci * 128:(ci + 1) * 128, h2 * XW:(h2 + 1) * XW]))
        prog.emit("sync", fxl)

    brow_wps = []
    for r in range(ntt):
        em, wp = ctx.Ls.issue()
        brow_wps.append(wp)

        def fb(s, r=r, em=em):
            em(s, lambda: s.dma_start(brow[0:1, r * 8:(r + 1) * 8], Bias_d[0:1, 0:8]))
        prog.emit("sync", fb)

    ats = [ctx.mm.inc() for _ in range(ngrp)]
    lg_c = ats[-1][1]
    shared_readers = [(ctx.ss, ctx.ss.n), (ctx.vv, ctx.vv.n)]
    ntt_h = XW // 128

    def frt(t, lrt=_norm([lrt_wp]), ats=ats, xl=[_norm([w]) for w in xl_wps],
            pf=list(psum_free) + _norm(shared_readers)):
        wait_pairs(t, pf)
        wait_pairs(t, lrt)
        for gi in range(ngrp):
            ci, h2 = gi // nxh, gi % nxh
            wait_pairs(t, xl[gi])
            last = None
            for tl in range(ntt_h):
                tt = h2 * ntt_h + tl
                last = t.matmul(ctx.pball[:, tt * 8:(tt + 1) * 8],
                                xf32[:, tl * 128:(tl + 1) * 128],
                                rt_sb[:, ci * 8:(ci + 1) * 8],
                                start=(gi == 0 and tl == 0),
                                stop=(gi == ngrp - 1 and tl == ntt_h - 1),
                                skip_group_check=True)
            ats[gi][0](last)
    prog.emit("tensor", frt)

    at, sg_c = ctx.ss.inc()

    def fsg(s, c=lg_c, at=at):
        s.wait_ge(ctx.mm.h, c)
        at(s.activation(s_tm[:], ctx.pball[:, 0:ntt * 8], AF.Sigmoid,
                        bias=ctx.zero_b[:, 0:1], scale=1.0))
    prog.emit("scalar", fsg)

    nbc = (ntt * 8 + 511) // 512
    at, bb_c = ctx.mm.inc()

    def fbb(t, c=_norm(brow_wps), sgc=sg_c, at=at):
        wait_pairs(t, c)
        t.wait_ge(ctx.ss.h, sgc)
        last = None
        for j in range(nbc):
            w = min(512, ntt * 8 - j * 512)
            last = t.matmul(ctx.pball[:, j * 512: j * 512 + w], ctx.ones_f32[0:1, :],
                            brow[0:1, j * 512: j * 512 + w], start=True, stop=True)
        at(last)
    prog.emit("tensor", fbb)

    fence_counts = [ctx.vv.inc() for _ in range(10)]
    at, w_c = ctx.vv.inc()

    def froute(v, c=bb_c, at=at, fences=fence_counts):
        def fence(i, instr):
            fences[i][0](instr)
            v.wait_ge(ctx.vv.h, fences[i][1])
        v.wait_ge(ctx.mm.h, c)
        fence(0, v.tensor_copy(et[:], ctx.pball[:, 0:ntt * 8]))
        fence(1, v.tensor_tensor(sp_tm[:], s_tm[:], et[:], ALU.add))
        fence(2, v.tensor_reduce(m1t[:], sp_tm[:].rearrange("p (g j) -> p g j", j=8),
                                 AX.X, ALU.max))
        last = None
        for j in range(8):
            last = v.tensor_tensor(et[:].rearrange("p (g j) -> p g j", j=8)[:, :, j],
                                   sp_tm[:].rearrange("p (g j) -> p g j", j=8)[:, :, j],
                                   m1t[:], ALU.is_ge)
        fence(3, last)
        fence(4, v.scalar_tensor_tensor(et[:], et[:], -BIG, sp_tm[:], ALU.mult, ALU.add))
        fence(5, v.tensor_reduce(m2t[:], et[:].rearrange("p (g j) -> p g j", j=8),
                                 AX.X, ALU.max))
        last = None
        for j in range(8):
            last = v.tensor_tensor(wmt[:].rearrange("p (g j) -> p g j", j=8)[:, :, j],
                                   sp_tm[:].rearrange("p (g j) -> p g j", j=8)[:, :, j],
                                   m2t[:], ALU.is_ge)
        fence(6, last)
        fence(7, v.tensor_tensor(wmt[:], wmt[:], s_tm[:], ALU.mult))
        fence(8, v.tensor_reduce(m1t[:], wmt[:].rearrange("p (g j) -> p g j", j=8),
                                 AX.X, ALU.add))
        v.tensor_scalar(m1t[:], m1t[:], EPS_MOE, None, ALU.add)
        v.wait_ge(ctx.vv.h, fences[8][1])
        fence(9, v.reciprocal(m2t[:], m1t[:]))
        last = None
        for j in range(8):
            last = v.tensor_tensor(wmt[:].rearrange("p (g j) -> p g j", j=8)[:, :, j],
                                   wmt[:].rearrange("p (g j) -> p g j", j=8)[:, :, j],
                                   m2t[:], ALU.mult)
        at(last)
    prog.emit("vector", froute)

    tr_ev = [0] * ntt
    wfm_war = [[], []]
    for tt in range(ntt):
        b = tt % 2
        at, trc = ctx.mm.inc()

        def ftr(t, tt=tt, b=b, c=w_c, prev=(tr_ev[tt - 2] if tt >= 2 else 0), at=at):
            if tt == 0:
                t.wait_ge(ctx.vv.h, c)
                wait_pairs(t, _norm([ctx.consts_l]))
            if prev:
                t.wait_ge(ctx.vv.h, prev)
            at(t.transpose(ctx.bank(b, p1=8, w=128), wmt[:, tt * 8:(tt + 1) * 8],
                           ctx.ident[:]))
        prog.emit("tensor", ftr)

        at, tec = ctx.vv.inc()

        def fte(v, tt=tt, b=b, c=trc, wfw=_norm(wfm_war[tt % 2]), at=at):
            v.wait_ge(ctx.mm.h, c)
            wait_pairs(v, wfw)
            at(v.tensor_copy(wfm[tt % 2][:], ctx.bank(b, p1=8, w=128)))
        prog.emit("vector", fte)
        tr_ev[tt] = tec

        emw, wfw_wp = ctx.Ls.issue()

        def fwd(s, tt=tt, c=tec, emw=emw):
            s.wait_ge(ctx.vv.h, c)
            emw(s, lambda: s.dma_start(
                Wrt_scratch[0:8, tt * 128:(tt + 1) * 128], wfm[tt % 2][:]))
        prog.emit("sync", fwd)
        wfm_war[tt % 2] = [wfw_wp]
    state["wrt_wp"] = ctx.Ls.finals()

    router_readers = [(ctx.vv, ctx.vv.n), (ctx.ss, sg_c)]
    for s_ in (sl_g, sl_u):
        for i in range(s_.n):
            s_.waits[i] = s_.waits[i] + _norm(router_readers)

    for eidx in range(1, len(order)):
        expert_body(eidx)

    lnr.br_default = lambda ci, tq: list(acc_last[tq])
    lnr.psum03 = [(ctx.mm, ctx.mm.n), (ctx.vv, ctx.vv.n), (ctx.ss, ctx.ss.n)]
    lnr.psum47 = [(ctx.mm, ctx.mm.n), (ctx.vv, ctx.vv.n)]
    return ctx.finals()



DEBUG_OUTS = set()


def build_program(T, blocks):
    nc = bass.Bass("TRN2", target_bir_lowering=False, detect_race_conditions=False)
    dbg = lambda n: ("ExternalOutput" if n in DEBUG_OUTS else "Internal")
    S_in = nc.dram_tensor("S_in", [D, T], F32, kind="ExternalInput").ap()
    S_out = nc.dram_tensor("S_out", [D, T], F32, kind="ExternalOutput").ap()
    ident = nc.dram_tensor("ident", [128, 128], F32, kind="ExternalInput").ap()
    br_kind = "ExternalInput" if "lnonly" in blocks else dbg("Br_scr")
    Br = nc.dram_tensor("Br_scr", [D, T], F32, kind=br_kind).ap()
    O_spill = nc.dram_tensor("O_scr", [D, T], BF16, kind="Internal").ap()
    Smid = [nc.dram_tensor(f"Smid{i}", [D, T], F32, kind="Internal").ap() for i in range(3)]
    Wrt = nc.dram_tensor("Wrt_scr", [8, T], BF16, kind=dbg("Wrt_scr")).ap()

    wd = {}
    if "attn0" in blocks:
        for n in ("l0_wqT", "l0_wkT", "l0_wvT", "l0_woT"):
            wd[n] = nc.dram_tensor(n, [D, D], F32, kind="ExternalInput").ap()
    if "ffn0" in blocks:
        for n in ("f0_wrT", "f0_wkT", "f0_wvT"):
            wd[n] = nc.dram_tensor(n, [D, D], F32, kind="ExternalInput").ap()
        wd["f0_mur"] = nc.dram_tensor("f0_mur", [D, 1], F32, kind="ExternalInput").ap()
        wd["f0_muk"] = nc.dram_tensor("f0_muk", [D, 1], F32, kind="ExternalInput").ap()
    if "attn1" in blocks:
        for n in ("l1_wq", "l1_wk", "l1_wv", "l1_wo"):
            wd[n] = nc.dram_tensor(n, [D, D], F32, kind="ExternalInput").ap()
    if "moe" in blocks:
        wd["m_rt"] = nc.dram_tensor("m_rt", [D, 8], F32, kind="ExternalInput").ap()
        wd["m_bias"] = nc.dram_tensor("m_bias", [1, 8], F32, kind="ExternalInput").ap()
        for e in range(NE):
            wd[f"m_wg{e}"] = nc.dram_tensor(f"m_wg{e}", [D, FF], F32, kind="ExternalInput").ap()
            wd[f"m_wu{e}"] = nc.dram_tensor(f"m_wu{e}", [D, FF], F32, kind="ExternalInput").ap()
            wd[f"m_wd{e}"] = nc.dram_tensor(f"m_wd{e}", [FF, D], F32, kind="ExternalInput").ap()
        wd["m_wgs"] = nc.dram_tensor("m_wgs", [D, FF], F32, kind="ExternalInput").ap()
        wd["m_wus"] = nc.dram_tensor("m_wus", [D, FF], F32, kind="ExternalInput").ap()
        wd["m_wds"] = nc.dram_tensor("m_wds", [FF, D], F32, kind="ExternalInput").ap()

    ctx = Ctx(nc, T)
    prog = Prog(nc)
    ctx.init_consts(prog, ident)

    import contextlib as _cl
    stream_in = S_in
    entry_wps = emit_entry(ctx, prog, stream_in)
    ready = {"all": list(entry_wps), "xbf": [list(entry_wps)] * (T // 512),
             "sout": [[]], "br_free": [], "psum": [], "psum47": []}

    def prep(bi, stW, wfree):
        blk = blocks[bi]
        if blk == "attn0":
            return alloc_attn_pre(ctx, stW, prog, wd["l0_wqT"], wd["l0_wkT"],
                                  wd["l0_wvT"], wfree)
        if blk == "attn1":
            return alloc_attn_pre(ctx, stW, prog, wd["l1_wq"], wd["l1_wk"],
                                  wd["l1_wv"], wfree)
        if blk == "ffn0":
            return alloc_ffn0_pre(ctx, stW, prog, wd["f0_wrT"], wd["f0_wkT"],
                                  wd["f0_wvT"], wd["f0_mur"], wd["f0_muk"], wfree)
        if blk == "moe":
            return alloc_moe_pre(ctx, stW, prog,
                                 [wd[f"m_wg{e}"] for e in range(NE)],
                                 [wd[f"m_wu{e}"] for e in range(NE)],
                                 [wd[f"m_wd{e}"] for e in range(NE)],
                                 wd["m_wgs"], wd["m_wus"], wd["m_wds"], wfree)
        if blk == "lnonly":
            return None
        raise ValueError(blk)

    stW = _cl.ExitStack()
    pre = prep(0, stW, wfree=[])

    for bi, blk in enumerate(blocks):
        is_last = (bi == len(blocks) - 1)
        s_out = S_out if is_last else Smid[bi]
        rs = 2.0 if blk == "attn1" else 1.0
        lnr = ResLN(ctx, prog, stream_in, Br, s_out, rs,
                    next_sbuf=(not is_last), ready=ready)
        with _cl.ExitStack() as st:
            if blk in ("attn0", "attn1"):
                wo = wd["l0_woT"] if blk == "attn0" else wd["l1_wo"]
                emit_attn(ctx, st, prog, wo, Br, O_spill,
                          gelu=(blk == "attn1"), ready=ready, pre=pre, lnr=lnr)
            elif blk == "ffn0":
                emit_ffn0(ctx, st, prog, Br, ready=ready, pre=pre, lnr=lnr)
            elif blk == "moe":
                emit_moe(ctx, st, prog, wd["m_rt"], wd["m_bias"],
                         [wd[f"m_wg{e}"] for e in range(NE)],
                         [wd[f"m_wu{e}"] for e in range(NE)],
                         [wd[f"m_wd{e}"] for e in range(NE)],
                         wd["m_wgs"], wd["m_wus"], wd["m_wds"], Wrt, Br,
                         stream_in, ready=ready, pre=pre, lnr=lnr)
            elif blk == "lnonly":
                pass
            else:
                raise ValueError(blk)
        stW.close()
        nstW = _cl.ExitStack()
        npre = prep(bi + 1, nstW, wfree=ctx.finals()) if bi + 1 < len(blocks) else None
        ready = lnr.finish()
        stW, pre = nstW, npre
        prog.flush(blk)
        stream_in = s_out
    stW.close()

    def ffin(s, fin=_norm(ctx.Ls.finals() + ctx.Lg.finals())):
        wait_pairs(s, fin)
    prog.emit("sync", ffin)
    prog.flush("fin")
    return nc



_IDENT = np.eye(128, dtype=np.float32)


def _prep_weights(kw):
    t = lambda a: np.ascontiguousarray(np.asarray(a).T)
    c = lambda a: np.ascontiguousarray(np.asarray(a))
    w = {}
    w["l0_wqT"] = t(kw["L0_attn_w_q"])
    w["l0_wkT"] = t(kw["L0_attn_w_k"])
    w["l0_wvT"] = t(kw["L0_attn_w_v"])
    w["l0_woT"] = t(kw["L0_attn_w_o"])
    w["f0_wrT"] = t(kw["L0_ffn_W_r"])
    w["f0_wkT"] = t(kw["L0_ffn_W_k"])
    w["f0_wvT"] = t(kw["L0_ffn_W_v"])
    w["f0_mur"] = c(kw["L0_ffn_mu_r"]).reshape(D, 1)
    w["f0_muk"] = c(kw["L0_ffn_mu_k"]).reshape(D, 1)
    w["l1_wq"] = c(kw["L1_attn_Wq"])
    w["l1_wk"] = c(kw["L1_attn_Wk"])
    w["l1_wv"] = c(kw["L1_attn_Wv"])
    w["l1_wo"] = c(kw["L1_attn_Wpost"])
    w["m_rt"] = c(kw["L1_ffn_W_gate_router"])
    w["m_bias"] = c(kw["L1_ffn_expert_bias"]).reshape(1, 8)
    for e in range(NE):
        w[f"m_wg{e}"] = c(kw["L1_ffn_Wg"][e])
        w[f"m_wu{e}"] = c(kw["L1_ffn_Wu"][e])
        w[f"m_wd{e}"] = c(kw["L1_ffn_Wd"][e])
    w["m_wgs"] = c(kw["L1_ffn_Wg_s"])
    w["m_wus"] = c(kw["L1_ffn_Wu_s"])
    w["m_wds"] = c(kw["L1_ffn_Wd_s"])
    w["ident"] = _IDENT
    return w


_BLOCK_WSETS = {
    "attn0": ["l0_wqT", "l0_wkT", "l0_wvT", "l0_woT"],
    "ffn0": ["f0_wrT", "f0_wkT", "f0_wvT", "f0_mur", "f0_muk"],
    "attn1": ["l1_wq", "l1_wk", "l1_wv", "l1_wo"],
    "moe": (["m_rt", "m_bias", "m_wgs", "m_wus", "m_wds"]
            + [f"m_w{k}{e}" for e in range(NE) for k in ("g", "u", "d")]),
}


def kernel(**inputs):
    T = 4096
    x = np.asarray(inputs["L0_attn_x"])
    B = x.shape[0]
    w = _prep_weights(inputs)

    stages = [["attn0", "ffn0", "attn1", "moe"]]
    S = [np.ascontiguousarray(x[b].T) for b in range(B)]
    for blocks in stages:
        nc = build_program(T, blocks)
        names = ["ident"] + sum((_BLOCK_WSETS[b] for b in blocks), [])
        in_maps = []
        for b in range(B):
            m = {"S_in": S[b]}
            for n in names:
                m[n] = w[n]
            in_maps.append(m)
        res = run_bass_kernel_spmd(nc, in_maps, core_ids=list(range(B)))
        S = [res.results[b]["S_out"] for b in range(B)]
    out = np.stack([S[b].T for b in range(B)], axis=0).astype(np.float32)
    return out


# version 25
# speedup vs baseline: 1.0512x; 1.0512x over previous
import numpy as np
import ml_dtypes
import concourse.bass as bass
import concourse.mybir as mybir
from concourse.bass_utils import run_bass_kernel_spmd

F32 = mybir.dt.float32
BF16 = mybir.dt.bfloat16
AF = mybir.ActivationFunctionType
ALU = mybir.AluOpType
AX = mybir.AxisListType

D = 1024
NH = 16
HD = 64
FF = 768
NE = 8
ATT_B = 10.0
EPS_LN = 1e-5
EPS_MOE = 1e-20
BIG = 1e30
ENGINES = ("tensor", "vector", "scalar", "gpsimd", "sync")


class Prog:
    def __init__(self, nc):
        self.nc = nc
        self.streams = {e: [] for e in ENGINES}

    def emit(self, eng, fn):
        self.streams[eng].append(fn)

    def flush(self, name=None):
        if not any(self.streams.values()):
            return
        with self.nc.Block(name) as blk:
            for eng in ENGINES:
                fns = self.streams[eng]
                if not fns:
                    continue

                def runner(e, fns=fns):
                    for fn in fns:
                        fn(e)
                getattr(blk, eng)(runner)
        self.streams = {e: [] for e in ENGINES}


class Sem:

    def __init__(self, nc, name):
        self.h = nc.alloc_semaphore(name)
        self.n = 0

    def inc(self, k=1):
        self.n += k
        h = self.h

        def attach(instr):
            instr.then_inc(h, k)
        return attach, self.n


class Slot:
    def __init__(self, n):
        self.n = n
        self.waits = [[] for _ in range(n)]

    def acquire(self, i):
        return i % self.n, list(self.waits[i % self.n])

    def release(self, i, pairs):
        self.waits[i % self.n] = [(s.h if isinstance(s, Sem) else s, c) for s, c in pairs]


def wait_pairs(e, pairs):
    for h, c in pairs:
        if c > 0:
            e.wait_ge(h.h if isinstance(h, Sem) else h, c)


def _norm(pairs):
    return [(s.h if isinstance(s, Sem) else s, c) for (s, c) in pairs]


class DmaLanes:

    def __init__(self, nc, name, n):
        self.sems = [Sem(nc, f"{name}{i}") for i in range(n)]
        self.n = n
        self.i = 0

    def issue(self, k=16):
        lane = self.sems[self.i % self.n]
        self.i += 1
        prev = lane.n
        at, cnt = lane.inc(k)

        def emit(eng, instr_fn):
            if prev:
                eng.wait_ge(lane.h, prev)
            at(instr_fn())
        return emit, (lane, cnt)

    def finals(self):
        return [(s, s.n) for s in self.sems if s.n > 0]


class Ctx:
    def __init__(self, nc, T):
        self.nc = nc
        self.T = T
        for nm in ("ldh", "ldg", "sth", "stg", "mm", "vv", "ss", "pp", "qk", "ex", "av"):
            setattr(self, nm, Sem(nc, nm))
        self.Ls = DmaLanes(nc, "Ls", 12)
        self.Lg = DmaLanes(nc, "Lg", 8)
        self.x_bf = [nc.alloc_sbuf_tensor(f"x_bf{i}", [128, T], BF16) for i in range(8)]
        self.allones_bf = nc.alloc_sbuf_tensor("allones_bf", [128, 128], BF16)
        self.ones_f32 = nc.alloc_sbuf_tensor("ones_f32", [1, 128], F32)
        self.bias_att = nc.alloc_sbuf_tensor("bias_att", [128, 1], F32)
        self.zero_b = nc.alloc_sbuf_tensor("zero_b", [128, 1], F32)
        self.eps_b = nc.alloc_sbuf_tensor("eps_b", [128, 1], F32)
        self.ident = nc.alloc_sbuf_tensor("ident_sb", [128, 128], F32)
        self.pball = nc.alloc_psum_tensor("pball", [128, 4096], F32)
        self.ln_y = [nc.alloc_sbuf_tensor(f"ln_y{i}", [128, 512], F32) for i in range(8)]
        self.ln_br = [nc.alloc_sbuf_tensor(f"ln_br{i}", [128, 512], F32) for i in range(2)]
        self.ln_ybf = [nc.alloc_sbuf_tensor(f"ln_ybf{i}", [128, 512], BF16) for i in range(2)]
        self.ln_ysq = [nc.alloc_sbuf_tensor(f"ln_ysq{i}", [128, 512], BF16) for i in range(2)]
        self.ln_mt = nc.alloc_sbuf_tensor("ln_mt", [128, 512], F32)
        self.ln_e2 = nc.alloc_sbuf_tensor("ln_e2", [128, 512], F32)
        self.ln_rstd = nc.alloc_sbuf_tensor("ln_rstd", [128, 512], F32)
        self.ln_bln = nc.alloc_sbuf_tensor("ln_bln", [128, 512], F32)
        self.ln_xo = [nc.alloc_sbuf_tensor(f"ln_xo{i}", [128, 512], F32) for i in range(2)]
        self.ln_war_y = [[] for _ in range(8)]
        self.ln_sl_br = Slot(2)
        self.ln_sl_yb = Slot(2)
        self.ln_sl_xo = Slot(2)
        self.ln_it = 0
        self.ln_sout = {}
        self.ln_psum_war = []

    _uid = 0

    def uniq(self):
        Ctx._uid += 1
        return f"u{Ctx._uid}_"

    def bank(self, k, p0=0, p1=128, w=512):
        return self.pball[p0:p1, k * 512: k * 512 + w]

    def init_consts(self, prog, ident_dram):
        at, self.consts_v = self.vv.inc()

        def fv(v, at=at):
            v.memset(self.allones_bf[:], 1.0)
            v.memset(self.ones_f32[:], 1.0)
            v.memset(self.bias_att[:], -ATT_B)
            v.memset(self.zero_b[:], 0.0)
            at(v.memset(self.eps_b[:], EPS_LN))
        prog.emit("vector", fv)

        em, wp = self.Ls.issue()

        def fs(s, em=em):
            em(s, lambda: s.dma_start(self.ident[:], ident_dram))
        prog.emit("sync", fs)
        self.consts_l = wp

    def finals(self):
        return (self.Ls.finals() + self.Lg.finals() +
                [(s, s.n) for s in (self.mm, self.vv, self.ss, self.pp,
                                    self.qk, self.ex, self.av) if s.n > 0])

    def barrier(self, prog, pairs, engines=ENGINES):
        pairs = _norm(pairs)
        for eng in engines:
            def f(e, pairs=list(pairs)):
                wait_pairs(e, pairs)
            prog.emit(eng, f)



def load_w(ctx, prog, dram_ap, sb, rows, cols, wait=()):
    nci = rows // 128
    out_ap = sb[:].rearrange("p (ci d) -> p ci d", ci=nci)
    in_ap = dram_ap.rearrange("(ci p) d -> p ci d", p=128)
    em, wp = ctx.Lg.issue()

    def f(g, wait=_norm(wait), em=em):
        wait_pairs(g, wait)
        em(g, lambda: g.dma_start(out_ap, in_ap))
    prog.emit("gpsimd", f)
    return wp


def wt(sb, ci, dj, cols):
    return sb[:, ci * cols + dj * 128: ci * cols + dj * 128 + 128]



def emit_entry(ctx, prog, S_in):
    wps = []
    for ci in range(8):
        em, wp = ctx.Lg.issue()

        def f(g, ci=ci, em=em):
            em(g, lambda: g.dma_start(ctx.x_bf[ci][:], S_in[ci * 128:(ci + 1) * 128, :]))
        prog.emit("gpsimd", f)
        wps.append(wp)
    return wps



LN_MODE = "full"


class ResLN:

    def __init__(self, ctx, prog, S_in, Br, S_out, res_scale, next_sbuf, ready):
        self.ctx, self.prog = ctx, prog
        self.S_in, self.Br, self.S_out = S_in, Br, S_out
        self.res_scale = res_scale
        self.next_sbuf = next_sbuf
        self.nq = ctx.T // 512
        self.emitted = set()
        self.xbf_tq = [None] * self.nq
        self.sout_tq = [None] * self.nq
        self.sout_new = {}
        self.sout_prev = ctx.ln_sout
        self.psum47 = list(ready.get("psum47", []))
        self.psum03 = []
        self.xbf_free = []
        self.br_default = None
        self._first = True

    def emit_tq(self, tq, br_w=None):
        if tq in self.emitted:
            return
        self.emitted.add(tq)
        ctx, prog = self.ctx, self.prog
        TQ = 512
        S_in, Br, S_out = self.S_in, self.Br, self.S_out
        res_scale = self.res_scale
        passthru = (LN_MODE == "passthru")
        y_sb, br_ld = ctx.ln_y, ctx.ln_br
        ybf, ysq = ctx.ln_ybf, ctx.ln_ysq
        mt, e2, rstd, bln, xo = ctx.ln_mt, ctx.ln_e2, ctx.ln_rstd, ctx.ln_bln, ctx.ln_xo
        sl_br, sl_yb, sl_xo = ctx.ln_sl_br, ctx.ln_sl_yb, ctx.ln_sl_xo
        war_y = ctx.ln_war_y
        first = self._first
        self._first = False
        p47 = _norm(list(ctx.ln_psum_war) + list(self.psum47)) if first else []

        yready = [0] * 8
        for ci in range(8):
            it = ctx.ln_it
            ctx.ln_it += 1
            bsl, bwait = sl_br.acquire(it)
            em1, wp1 = ctx.Ls.issue()
            em2, wp2 = ctx.Ls.issue()

            def fl(s, ci=ci, tq=tq, bsl=bsl, bwait=bwait, wy=list(war_y[ci]),
                   sp=_norm(self.sout_prev.get((ci, tq), [])),
                   bw=_norm(br_w(ci) if br_w else []), em1=em1, em2=em2):
                wait_pairs(s, wy)
                wait_pairs(s, sp)
                em1(s, lambda: s.dma_start(y_sb[ci][:],
                                           S_in[ci * 128:(ci + 1) * 128, tq * TQ:(tq + 1) * TQ]))
                wait_pairs(s, bwait)
                wait_pairs(s, bw)
                em2(s, lambda: s.dma_start(br_ld[bsl][:],
                                           Br[ci * 128:(ci + 1) * 128, tq * TQ:(tq + 1) * TQ]))
            prog.emit("sync", fl)

            at, yc = ctx.vv.inc()

            def fv(v, ci=ci, lwp=_norm([wp1, wp2]), bsl=bsl, at=at):
                wait_pairs(v, lwp)
                if res_scale == 1.0:
                    at(v.tensor_tensor(y_sb[ci][:], y_sb[ci][:], br_ld[bsl][:], ALU.add))
                else:
                    at(v.scalar_tensor_tensor(y_sb[ci][:], y_sb[ci][:], float(res_scale),
                                              br_ld[bsl][:], ALU.mult, ALU.add))
            prog.emit("vector", fv)
            yready[ci] = yc
            sl_br.release(it, [(ctx.vv, yc)])

            ysl, ywait = sl_yb.acquire(it)
            if not passthru:
                atc, _ = ctx.ss.inc()
                at, sqc = ctx.ss.inc()

                def fs(s, ci=ci, yc=yc, ysl=ysl, ywait=ywait, at=at, atc=atc):
                    s.wait_ge(ctx.vv.h, yc)
                    wait_pairs(s, ywait)
                    atc(s.copy(ybf[ysl][:], y_sb[ci][:]))
                    at(s.square(ysq[ysl][:], y_sb[ci][:]))
                prog.emit("scalar", fs)

                at, mmc = ctx.mm.inc()

                def ft(t, ci=ci, sqc=sqc, ysl=ysl,
                       pw=(p47 if ci == 0 else []),
                       cw=(ctx.consts_v if first and ci == 0 else 0), at=at):
                    t.wait_ge(ctx.ss.h, sqc)
                    if cw:
                        t.wait_ge(ctx.vv.h, cw)
                    wait_pairs(t, pw)
                    t.matmul(ctx.bank(4), ctx.allones_bf[:], ybf[ysl][:],
                             start=(ci == 0), stop=(ci == 7), skip_group_check=True)
                    last = t.matmul(ctx.bank(6), ctx.allones_bf[:], ysq[ysl][:],
                                    start=(ci == 0), stop=(ci == 7), skip_group_check=True)
                    at(last)
                prog.emit("tensor", ft)
                sl_yb.release(it, [(ctx.mm, mmc)])
        stats_c = ctx.mm.n
        if passthru:
            at, ab_c = ctx.vv.inc()

            def fpt(v, at=at):
                v.memset(rstd[:], 1.0)
                at(v.memset(bln[:], 0.0))
            prog.emit("vector", fpt)
        else:
            fv1 = ctx.vv.inc()
            fv2 = ctx.vv.inc()
            fv3 = ctx.vv.inc()
            at, var_c = ctx.vv.inc()

            def fsv(v, c=stats_c, at=at, fv1=fv1, fv2=fv2, fv3=fv3):
                v.wait_ge(ctx.mm.h, c)
                fv1[0](v.tensor_scalar(mt[:], ctx.pball[:, 2048:2048 + TQ], 1.0 / D,
                                       None, ALU.mult))
                fv2[0](v.tensor_scalar(e2[:], ctx.pball[:, 3072:3072 + TQ], 1.0 / D,
                                       None, ALU.mult))
                v.wait_ge(ctx.vv.h, fv2[1])
                fv3[0](v.tensor_tensor(bln[:], mt[:], mt[:], ALU.mult))
                v.wait_ge(ctx.vv.h, fv3[1])
                at(v.tensor_tensor(e2[:], e2[:], bln[:], ALU.subtract))
            prog.emit("vector", fsv)

            fsl_ = ctx.ss.inc()
            at, rstd_c = ctx.ss.inc()

            def fss(s, c=var_c, at=at, fsl_=fsl_):
                s.wait_ge(ctx.vv.h, c)
                fsl_[0](s.activation(rstd[:], e2[:], AF.Ln, bias=ctx.eps_b[:, 0:1],
                                     scale=1.0))
                s.wait_ge(ctx.ss.h, fsl_[1])
                at(s.activation(rstd[:], rstd[:], AF.Exp, bias=ctx.zero_b[:, 0:1],
                                scale=-0.5))
            prog.emit("scalar", fss)

            at, ab_c = ctx.vv.inc()

            def fbv(v, c=rstd_c, at=at):
                v.wait_ge(ctx.ss.h, c)
                at(v.scalar_tensor_tensor(bln[:], mt[:], -1.0, rstd[:], ALU.mult, ALU.mult))
            prog.emit("vector", fbv)

        for ci in range(8):
            it = ctx.ln_it
            ctx.ln_it += 1
            xsl, xwait = sl_xo.acquire(it)
            atf = ctx.vv.inc()
            at, xoc = ctx.vv.inc()

            def fa(v, ci=ci, c=ab_c, xsl=xsl, xwait=xwait, at=at, atf=atf):
                v.wait_ge(ctx.vv.h, c)
                wait_pairs(v, xwait)
                atf[0](v.tensor_tensor(xo[xsl][:], y_sb[ci][:], rstd[:], ALU.mult))
                v.wait_ge(ctx.vv.h, atf[1])
                at(v.tensor_tensor(xo[xsl][:], xo[xsl][:], bln[:], ALU.add))
            prog.emit("vector", fa)
            war_y[ci] = [(ctx.vv, xoc)]

            ems, wps = ctx.Ls.issue()

            def fst(s, ci=ci, tq=tq, c=xoc, xsl=xsl, ems=ems):
                s.wait_ge(ctx.vv.h, c)
                ems(s, lambda: s.dma_start(
                    S_out[ci * 128:(ci + 1) * 128, tq * TQ:(tq + 1) * TQ], xo[xsl][:]))
            prog.emit("sync", fst)
            self.sout_new[(ci, tq)] = [wps]
            consumers = [wps]

            if self.next_sbuf:
                at, xbc = ctx.pp.inc()

                def fc(g, ci=ci, tq=tq, c=xoc, xsl=xsl,
                       xf=_norm(self.xbf_free) if ci == 0 else [], at=at):
                    g.wait_ge(ctx.vv.h, c)
                    wait_pairs(g, xf)
                    at(g.tensor_copy(ctx.x_bf[ci][:, tq * TQ:(tq + 1) * TQ], xo[xsl][:]))
                prog.emit("gpsimd", fc)
                consumers.append((ctx.pp, xbc))
            sl_xo.release(it, consumers)
        self.xbf_tq[tq] = [(ctx.pp, ctx.pp.n)]
        self.sout_tq[tq] = ctx.Ls.finals()

    def finish(self):
        ctx = self.ctx
        for tq in range(self.nq):
            bd = self.br_default
            self.emit_tq(tq, br_w=((lambda ci, tq=tq: bd(ci, tq)) if bd else None))
        ctx.ln_sout = self.sout_new
        ctx.ln_psum_war = [(ctx.vv, ctx.vv.n)]
        done = ctx.Ls.finals() + [(ctx.vv, ctx.vv.n), (ctx.mm, ctx.mm.n),
                                  (ctx.ss, ctx.ss.n)]
        if self.next_sbuf:
            done.append((ctx.pp, ctx.pp.n))
        return {"all": done, "xbf": self.xbf_tq, "sout": self.sout_tq,
                "br_free": [(ctx.vv, ctx.vv.n)],
                "psum": list(self.psum03),
                "psum47": [(ctx.vv, ctx.vv.n)]}



def alloc_attn_pre(ctx, stW, prog, Wq_d, Wk_d, Wv_d, wfree):
    u = ctx.uniq()
    nc, T = ctx.nc, ctx.T
    ntt = T // 128
    pre = {}
    pre["wq"] = stW.enter_context(nc.sbuf_tensor(u + "wq", [128, 8192], BF16))
    pre["wk"] = stW.enter_context(nc.sbuf_tensor(u + "wk", [128, 8192], BF16))
    pre["wv"] = stW.enter_context(nc.sbuf_tensor(u + "wv", [128, 8192], BF16))
    pre["Qg"] = stW.enter_context(nc.sbuf_tensor(u + "Qg", [128, T], BF16))
    pre["Kg"] = stW.enter_context(nc.sbuf_tensor(u + "Kg", [128, T], BF16))
    pre["Vg"] = stW.enter_context(nc.sbuf_tensor(u + "Vg", [128, ntt * 130], BF16))
    load_w(ctx, prog, Wq_d, pre["wq"], D, D, wait=wfree)
    load_w(ctx, prog, Wk_d, pre["wk"], D, D, wait=wfree)
    pre["lw"] = load_w(ctx, prog, Wv_d, pre["wv"], D, D, wait=wfree)
    at, c = ctx.vv.inc()

    def fones(v, at=at, wfree=_norm(wfree)):
        wait_pairs(v, wfree)
        at(v.memset(pre["Vg"][:].rearrange("p (tt g c) -> p tt g c", g=2, c=65)
                    [:, :, :, 64:65], 1.0))
    prog.emit("vector", fones)
    pre["vones"] = c
    return pre


def emit_attn(ctx, st, prog, Wo_d, Br, O_spill, gelu, ready, pre, lnr):
    u = ctx.uniq()
    nc, T = ctx.nc, ctx.T
    nqc = T // 512
    ntt = T // 128
    nkt = ntt

    import contextlib as _cl
    stA = st.enter_context(_cl.ExitStack())
    wq_sb, wk_sb, wv_sb = pre["wq"], pre["wk"], pre["wv"]
    Q_grp, K_grp, V_grp = pre["Qg"], pre["Kg"], pre["Vg"]
    lw = pre["lw"]
    vones_c = pre["vones"]
    P_T = stA.enter_context(nc.sbuf_tensor(u + "PT", [128, 2048], BF16))
    o_raw = stA.enter_context(nc.sbuf_tensor(u + "oraw", [128, T], BF16))
    O_grp = stA.enter_context(nc.sbuf_tensor(u + "Og", [128, T], BF16))
    dn_sb = stA.enter_context(nc.sbuf_tensor(u + "dn", [1, 2048], F32))
    rc_sb = dn_sb
    xbf = ready["xbf"]
    psum_free = _norm(ready.get("psum", []))
    br_free = _norm(ready.get("br_free", []))

    sl_pj = Slot(3)
    for i in range(3):
        sl_pj.waits[i] = list(psum_free)
    sl_dn = Slot(2)
    war_Q = []
    war_O = []
    war_oraw = []
    ogc_wps = []
    pjit = 0
    psum47_free = psum_free + _norm(ready.get("psum47", []))
    score_war = [list(psum_free), list(psum_free)]
    pt_war = [[], []]
    war_ob = list(psum47_free)
    war_rb = list(psum47_free)

    for hp in range(8):
        for wsb, dst in ((wq_sb, Q_grp), (wk_sb, K_grp)):
            for tc in range(nqc):
                b, bwait = sl_pj.acquire(pjit)
                at, mmc = ctx.mm.inc()

                def fmm(t, b=b, bwait=bwait, wsb=wsb, tc=tc, hp=hp, lw=_norm([lw]),
                        wq=list(war_Q), xw=(_norm(xbf[tc]) if hp == 0 else []), at=at):
                    wait_pairs(t, lw)
                    wait_pairs(t, xw)
                    wait_pairs(t, bwait)
                    wait_pairs(t, wq)
                    last = None
                    for ci in range(8):
                        last = t.matmul(ctx.bank(b),
                                        wsb[:, ci * 1024 + hp * 128: ci * 1024 + hp * 128 + 128],
                                        ctx.x_bf[ci][:, tc * 512:(tc + 1) * 512],
                                        start=(ci == 0), stop=(ci == 7))
                    at(last)
                prog.emit("tensor", fmm)

                at, evc = ctx.vv.inc()

                def fev(v, b=b, c=mmc, dst=dst, tc=tc, at=at):
                    v.wait_ge(ctx.mm.h, c)
                    at(v.tensor_copy(dst[:, tc * 512:(tc + 1) * 512], ctx.bank(b)))
                prog.emit("vector", fev)
                sl_pj.release(pjit, [(ctx.vv, evc)])
                pjit += 1

        for tt in range(ntt):
            b, bwait = sl_pj.acquire(pjit)
            at, mmc = ctx.mm.inc()

            def fmm(t, b=b, bwait=bwait, tt=tt, hp=hp, wq=list(war_Q), vc=vones_c,
                    xw=(_norm(xbf[tt // 4]) if hp == 0 else []), at=at):
                wait_pairs(t, bwait)
                wait_pairs(t, wq)
                wait_pairs(t, xw)
                if tt == 0:
                    t.wait_ge(ctx.vv.h, vc)
                last = None
                for ci in range(8):
                    last = t.matmul(ctx.bank(b, w=128),
                                    ctx.x_bf[ci][:, tt * 128:(tt + 1) * 128],
                                    wv_sb[:, ci * 1024 + hp * 128: ci * 1024 + hp * 128 + 128],
                                    start=(ci == 0), stop=(ci == 7))
                at(last)
            prog.emit("tensor", fmm)

            at, evc = ctx.vv.inc()

            def fev(v, b=b, c=mmc, tt=tt, at=at):
                v.wait_ge(ctx.mm.h, c)
                src = ctx.bank(b, w=128).rearrange("p (g c) -> p g c", g=2)
                dst = V_grp[:, tt * 130: tt * 130 + 130].rearrange(
                    "p (g c) -> p g c", c=65)[:, :, 0:64]
                at(v.tensor_copy(dst, src))
            prog.emit("vector", fev)
            sl_pj.release(pjit, [(ctx.vv, evc)])
            pjit += 1
        if hp == 7:
            lnr.xbf_free = [(ctx.mm, ctx.mm.n)]
        proj_done = (ctx.vv, ctx.vv.n)
        score_war[0] = score_war[0] + [proj_done]
        score_war[1] = score_war[1] + [proj_done]
        war_ob = war_ob + [proj_done]

        norm_last = 0
        pend = None
        for qc in range(nqc):
            ob_wait = list(war_ob)
            qk_counts = {}

            def emit_qk(kt, first):
                g = kt % 2
                sg_wait = list(score_war[g])
                at, qkc = ctx.qk.inc()
                qk_counts[kt] = qkc

                def fqk(t, g=g, kt=kt, qc=qc, sg_wait=sg_wait,
                        ob_wait=(ob_wait if first else []), at=at):
                    wait_pairs(t, sg_wait)
                    wait_pairs(t, ob_wait)
                    t.matmul(ctx.pball[:, g * 1024: g * 1024 + 512],
                             K_grp[0:64, kt * 128:(kt + 1) * 128],
                             Q_grp[0:64, qc * 512:(qc + 1) * 512],
                             start=True, stop=True)
                    at(t.matmul(ctx.pball[:, g * 1024 + 512: g * 1024 + 1024],
                                K_grp[64:128, kt * 128:(kt + 1) * 128],
                                Q_grp[64:128, qc * 512:(qc + 1) * 512],
                                start=True, stop=True))
                prog.emit("tensor", fqk)

            emit_qk(0, True)
            emit_qk(1, False)
            for kt in range(nkt):
                g = kt % 2
                pt_wait = list(pt_war[g])
                at, exc = ctx.ex.inc()

                def fex(s, g=g, c=qk_counts[kt], pt_wait=pt_wait, at=at):
                    s.wait_ge(ctx.qk.h, c)
                    wait_pairs(s, pt_wait)
                    at(s.activation(P_T[:, g * 1024:(g + 1) * 1024],
                                    ctx.pball[:, g * 1024:(g + 1) * 1024],
                                    AF.Exp, bias=ctx.bias_att[:, 0:1], scale=0.125))
                prog.emit("scalar", fex)
                score_war[g] = [(ctx.ex, exc)]

                at, avc = ctx.av.inc()

                def fav(t, g=g, kt=kt, c=exc, at=at):
                    t.wait_ge(ctx.ex.h, c)
                    t.matmul(ctx.bank(4, p1=65), V_grp[:, kt * 130: kt * 130 + 65],
                             P_T[:, g * 1024: g * 1024 + 512],
                             start=(kt == 0), stop=(kt == nkt - 1))
                    at(t.matmul(ctx.bank(5, p1=65), V_grp[:, kt * 130 + 65: kt * 130 + 130],
                                P_T[:, g * 1024 + 512: g * 1024 + 1024],
                                start=(kt == 0), stop=(kt == nkt - 1)))
                prog.emit("tensor", fav)
                pt_war[g] = [(ctx.av, avc)]

                if kt + 2 < nkt:
                    emit_qk(kt + 2, False)

                if kt == 2 and pend is not None:
                    pqc, pdsl, prcc = pend
                    rbw = list(war_rb)
                    at, rbc = ctx.mm.inc()

                    def frb(t, c=prcc, rbw=rbw, pdsl=pdsl, at=at):
                        t.wait_ge(ctx.ss.h, c)
                        wait_pairs(t, rbw)
                        t.matmul(ctx.bank(6, p1=64), ctx.ones_f32[0:1, 0:64],
                                 rc_sb[0:1, pdsl * 1024: pdsl * 1024 + 512],
                                 start=True, stop=True)
                        at(t.matmul(ctx.bank(7, p1=64), ctx.ones_f32[0:1, 0:64],
                                    rc_sb[0:1, pdsl * 1024 + 512: pdsl * 1024 + 1024],
                                    start=True, stop=True))
                    prog.emit("tensor", frb)
                    sl_dn.release(pdsl, [(ctx.mm, rbc)])

                    at, nmc = ctx.vv.inc()

                    def fnm(v, c=rbc, pqc=pqc, wo=list(war_O if not gelu else []), at=at):
                        v.wait_ge(ctx.mm.h, c)
                        wait_pairs(v, wo)
                        srcA = o_raw[0:64, pqc * 512:(pqc + 1) * 512]
                        srcB = o_raw[64:128, pqc * 512:(pqc + 1) * 512]
                        if gelu:
                            v.tensor_tensor(srcA, srcA, ctx.bank(6, p1=64), ALU.mult)
                            at(v.tensor_tensor(srcB, srcB, ctx.bank(7, p1=64), ALU.mult))
                        else:
                            v.tensor_tensor(O_grp[0:64, pqc * 512:(pqc + 1) * 512],
                                            srcA, ctx.bank(6, p1=64), ALU.mult)
                            at(v.tensor_tensor(O_grp[64:128, pqc * 512:(pqc + 1) * 512],
                                               srcB, ctx.bank(7, p1=64), ALU.mult))
                    prog.emit("vector", fnm)
                    norm_last = nmc
                    war_rb = [(ctx.vv, nmc)]
                    pend = None
            av_all = ctx.av.n

            dsl, dwait = sl_dn.acquire(qc)
            atd = ctx.vv.inc()
            at, evc = ctx.vv.inc()

            def fdn(v, c=av_all, dsl=dsl, dwait=dwait, qc=qc,
                    woraw=list(war_oraw), at=at, atd=atd):
                v.wait_ge(ctx.av.h, c)
                wait_pairs(v, dwait)
                wait_pairs(v, woraw)
                v.tensor_copy(dn_sb[0:1, dsl * 1024: dsl * 1024 + 512],
                              ctx.bank(4, p0=64, p1=65))
                atd[0](v.tensor_copy(dn_sb[0:1, dsl * 1024 + 512: dsl * 1024 + 1024],
                                     ctx.bank(5, p0=64, p1=65)))
                v.tensor_copy(o_raw[0:64, qc * 512:(qc + 1) * 512], ctx.bank(4, p1=64))
                at(v.tensor_copy(o_raw[64:128, qc * 512:(qc + 1) * 512],
                                 ctx.bank(5, p1=64)))
            prog.emit("vector", fdn)
            war_ob = [(ctx.vv, evc)]

            atl = ctx.ss.inc()
            at, rcc = ctx.ss.inc()

            def frc(s, dsl=dsl, c=evc, at=at, atl=atl):
                s.wait_ge(ctx.vv.h, c)
                atl[0](s.activation(rc_sb[0:1, dsl * 1024:(dsl + 1) * 1024],
                                    dn_sb[0:1, dsl * 1024:(dsl + 1) * 1024],
                                    AF.Ln, bias=ctx.zero_b[0:1, 0:1], scale=1.0))
                s.wait_ge(ctx.ss.h, atl[1])
                at(s.activation(rc_sb[0:1, dsl * 1024:(dsl + 1) * 1024],
                                rc_sb[0:1, dsl * 1024:(dsl + 1) * 1024],
                                AF.Exp, bias=ctx.zero_b[0:1, 0:1], scale=-1.0))
            prog.emit("scalar", frc)
            pend = (qc, dsl, rcc)

        if pend is not None:
            pqc, pdsl, prcc = pend
            rbw = list(war_rb)
            at, rbc = ctx.mm.inc()

            def frb(t, c=prcc, rbw=rbw, pdsl=pdsl, at=at):
                t.wait_ge(ctx.ss.h, c)
                wait_pairs(t, rbw)
                t.matmul(ctx.bank(6, p1=64), ctx.ones_f32[0:1, 0:64],
                         rc_sb[0:1, pdsl * 1024: pdsl * 1024 + 512],
                         start=True, stop=True)
                at(t.matmul(ctx.bank(7, p1=64), ctx.ones_f32[0:1, 0:64],
                            rc_sb[0:1, pdsl * 1024 + 512: pdsl * 1024 + 1024],
                            start=True, stop=True))
            prog.emit("tensor", frb)
            sl_dn.release(pdsl, [(ctx.mm, rbc)])

            at, nmc = ctx.vv.inc()

            def fnm(v, c=rbc, pqc=pqc, wo=list(war_O if not gelu else []), at=at):
                v.wait_ge(ctx.mm.h, c)
                wait_pairs(v, wo)
                srcA = o_raw[0:64, pqc * 512:(pqc + 1) * 512]
                srcB = o_raw[64:128, pqc * 512:(pqc + 1) * 512]
                if gelu:
                    v.tensor_tensor(srcA, srcA, ctx.bank(6, p1=64), ALU.mult)
                    at(v.tensor_tensor(srcB, srcB, ctx.bank(7, p1=64), ALU.mult))
                else:
                    v.tensor_tensor(O_grp[0:64, pqc * 512:(pqc + 1) * 512],
                                    srcA, ctx.bank(6, p1=64), ALU.mult)
                    at(v.tensor_tensor(O_grp[64:128, pqc * 512:(pqc + 1) * 512],
                                       srcB, ctx.bank(7, p1=64), ALU.mult))
            prog.emit("vector", fnm)
            norm_last = nmc
            war_rb = [(ctx.vv, nmc)]
            pend = None

        core_scores_done = [(ctx.ex, ctx.ex.n)]
        for sidx in range(3):
            sl_pj.waits[sidx] = _norm(core_scores_done)

        if gelu:
            at, gc = ctx.ss.inc()

            def fg(s, c=norm_last, wo=list(war_O), at=at):
                s.wait_ge(ctx.vv.h, c)
                wait_pairs(s, wo)
                at(s.activation(O_grp[:], o_raw[:], AF.Gelu,
                                bias=ctx.zero_b[:, 0:1], scale=1.0))
            prog.emit("scalar", fg)
            og_ready = [(ctx.ss, gc)]
        else:
            og_ready = [(ctx.vv, norm_last)]

        emo, ogwp = ctx.Ls.issue()

        def fsp(s, hp=hp, og=list(og_ready), emo=emo):
            wait_pairs(s, og)
            emo(s, lambda: s.dma_start(O_spill[hp * 128:(hp + 1) * 128, :], O_grp[:]))
        prog.emit("sync", fsp)
        ogc_wps.append(ogwp)
        war_O = [ogwp]
        war_oraw = _norm(og_ready)
        war_Q = [(ctx.qk, ctx.qk.n), (ctx.av, ctx.av.n)]

    lnr.psum47 = [(ctx.vv, ctx.vv.n), (ctx.ss, ctx.ss.n)]
    stA.close()
    wo_sb = wq_sb
    lwo = load_w(ctx, prog, Wo_d, wo_sb, D, D, wait=[(ctx.qk, ctx.qk.n)])
    spill_all = list(ogc_wps)

    nth = max(1, T // 1024)
    THW = min(1024, T)
    O_ld = [st.enter_context(nc.sbuf_tensor(u + f"old{i}", [128, THW], BF16)) for i in range(8)]
    brst = [st.enter_context(nc.sbuf_tensor(u + f"brst{i}", [128, THW], F32)) for i in range(2)]

    sl_br = Slot(2)
    war_old = []
    pend_ln = None
    ntq_th = THW // 512
    for th in range(nth):
        ol_wps = []
        for ci in range(8):
            em, wp = ctx.Ls.issue()
            ol_wps.append(wp)

            def fol(s, ci=ci, th=th, wo=list(war_old),
                    spill=_norm(spill_all) if th == 0 and ci == 0 else [], em=em):
                wait_pairs(s, spill)
                if ci == 0:
                    wait_pairs(s, wo)
                em(s, lambda: s.dma_start(
                    O_ld[ci][:],
                    O_spill[ci * 128:(ci + 1) * 128, th * THW:(th + 1) * THW]))
            prog.emit("sync", fol)

        if pend_ln is not None:
            pth, pwps = pend_ln
            for tq in range(pth * ntq_th, (pth + 1) * ntq_th):
                lnr.emit_tq(tq, br_w=lambda ci, pwps=pwps: [pwps[ci]])
            pend_ln = None

        wps_th = {}
        for dj in range(8):
            slot, swait = sl_br.acquire(th * 8 + dj)
            evc = 0
            for tc in range(THW // 512):
                b, bwait = sl_pj.acquire(pjit)
                at, mmc = ctx.mm.inc()

                def fmm(t, b=b, bwait=bwait, dj=dj, tc=tc,
                        olc=_norm(ol_wps), lwo=_norm([lwo]), at=at):
                    wait_pairs(t, olc)
                    wait_pairs(t, lwo)
                    wait_pairs(t, bwait)
                    last = None
                    for ci in range(8):
                        last = t.matmul(ctx.bank(b), wt(wo_sb, ci, dj, 1024),
                                        O_ld[ci][:, tc * 512:(tc + 1) * 512],
                                        start=(ci == 0), stop=(ci == 7))
                    at(last)
                prog.emit("tensor", fmm)

                at, evc = ctx.vv.inc()

                def fev(v, b=b, c=mmc, slot=slot, tc=tc,
                        swait=(swait if tc == 0 else []), at=at):
                    v.wait_ge(ctx.mm.h, c)
                    wait_pairs(v, swait)
                    at(v.tensor_copy(brst[slot][:, tc * 512:(tc + 1) * 512], ctx.bank(b)))
                prog.emit("vector", fev)
                sl_pj.release(pjit, [(ctx.vv, evc)])
                pjit += 1

            em, wp = ctx.Ls.issue()

            def fst(s, dj=dj, th=th, slot=slot, c=evc, em=em,
                    bf=(br_free if th == 0 and dj == 0 else [])):
                s.wait_ge(ctx.vv.h, c)
                wait_pairs(s, bf)
                em(s, lambda: s.dma_start(
                    Br[dj * 128:(dj + 1) * 128, th * THW:(th + 1) * THW], brst[slot][:]))
            prog.emit("sync", fst)
            wps_th[dj] = wp
            sl_br.release(th * 8 + dj, [wp])
        pend_ln = (th, wps_th)
        war_old = [(ctx.mm, ctx.mm.n)]

    lnr.psum03 = [(ctx.mm, ctx.mm.n), (ctx.vv, ctx.vv.n), (ctx.ex, ctx.ex.n)]
    if pend_ln is not None:
        pth, pwps = pend_ln
        for tq in range(pth * ntq_th, (pth + 1) * ntq_th):
            lnr.emit_tq(tq, br_w=lambda ci, pwps=pwps: [pwps[ci]])
    return ctx.finals()



def alloc_ffn0_pre(ctx, stW, prog, Wr_d, Wk_d, Wv_d, mur_d, muk_d, wfree):
    u = ctx.uniq()
    nc = ctx.nc
    pre = {}
    pre["wr"] = stW.enter_context(nc.sbuf_tensor(u + "fwr", [128, 8192], BF16))
    pre["wk"] = stW.enter_context(nc.sbuf_tensor(u + "fwk", [128, 8192], BF16))
    pre["wv"] = stW.enter_context(nc.sbuf_tensor(u + "fwv", [128, 8192], BF16))
    pre["mur"] = stW.enter_context(nc.sbuf_tensor(u + "mur", [128, 8], F32))
    pre["muk"] = stW.enter_context(nc.sbuf_tensor(u + "muk", [128, 8], F32))
    load_w(ctx, prog, Wr_d, pre["wr"], D, D, wait=wfree)
    load_w(ctx, prog, Wk_d, pre["wk"], D, D, wait=wfree)
    pre["lw"] = load_w(ctx, prog, Wv_d, pre["wv"], D, D, wait=wfree)
    mu_wps = []
    for ci in range(8):
        em1, wp1 = ctx.Ls.issue()
        em2, wp2 = ctx.Ls.issue()
        mu_wps += [wp1, wp2]

        def fmu(s, ci=ci, em1=em1, em2=em2, wfree=_norm(wfree if ci == 0 else [])):
            wait_pairs(s, wfree)
            em1(s, lambda: s.dma_start(pre["mur"][:, ci:ci + 1],
                                       mur_d[ci * 128:(ci + 1) * 128, 0:1]))
            em2(s, lambda: s.dma_start(pre["muk"][:, ci:ci + 1],
                                       muk_d[ci * 128:(ci + 1) * 128, 0:1]))
        prog.emit("sync", fmu)
    pre["mu_wps"] = mu_wps
    return pre


def emit_ffn0(ctx, st, prog, Br, ready, pre, lnr):
    u = ctx.uniq()
    nc, T = ctx.nc, ctx.T
    TQ = min(512, T)
    nq = T // TQ
    nsub = TQ // 512

    wr_sb, wk_sb, wv_sb = pre["wr"], pre["wk"], pre["wv"]
    mur, muk = pre["mur"], pre["muk"]
    lw = pre["lw"]
    mu_wps = pre["mu_wps"]
    xbf = ready["xbf"]
    psum_free = _norm(ready.get("psum", []))
    br_free = _norm(ready.get("br_free", []))
    xr_bf = [st.enter_context(nc.sbuf_tensor(u + f"xr{i}", [128, TQ], BF16)) for i in range(8)]
    xk_bf = [st.enter_context(nc.sbuf_tensor(u + f"xk{i}", [128, TQ], BF16)) for i in range(8)]
    dtmp = st.enter_context(nc.sbuf_tensor(u + "dtmp", [128, TQ], BF16))
    r_grp = st.enter_context(nc.sbuf_tensor(u + "rgrp", [128, 8 * TQ], BF16))
    ka_grp = st.enter_context(nc.sbuf_tensor(u + "kagrp", [128, 8 * TQ], BF16))
    kmx = st.enter_context(nc.sbuf_tensor(u + "kmx", [128, 512], BF16))
    brst = [st.enter_context(nc.sbuf_tensor(u + f"fbr{i}", [128, TQ], F32)) for i in range(2)]

    sl_mm = Slot(3)
    for i in range(3):
        sl_mm.waits[i] = list(psum_free)
    sl_br = Slot(2)
    pjit = 0
    war_xr = []

    tpc = TQ // 512
    pend_ln = None
    for tq in range(nq):
        g0 = tq * TQ
        for ci in range(8):
            at, xrc = ctx.vv.inc()

            def fx(v, ci=ci, g0=g0, wx=list(war_xr),
                   mc=(_norm(mu_wps) if tq == 0 and ci == 0 else []),
                   xw=_norm(xbf[(g0 + TQ) // 512 - 1]),
                   at=at):
                wait_pairs(v, mc)
                wait_pairs(v, xw)
                wait_pairs(v, wx)
                x = ctx.x_bf[ci]
                if g0 == 0:
                    v.tensor_copy(dtmp[:, 0:1], x[:, 0:1])
                    v.tensor_tensor(dtmp[:, 1:TQ], x[:, 1:TQ], x[:, 0:TQ - 1], ALU.subtract)
                    v.scalar_tensor_tensor(xr_bf[ci][:, 1:TQ], dtmp[:, 1:TQ],
                                           mur[:, ci:ci + 1], x[:, 0:TQ - 1],
                                           ALU.mult, ALU.add)
                    v.tensor_scalar(xr_bf[ci][:, 0:1], x[:, 0:1], mur[:, ci:ci + 1],
                                    None, ALU.mult)
                    v.scalar_tensor_tensor(xk_bf[ci][:, 1:TQ], dtmp[:, 1:TQ],
                                           muk[:, ci:ci + 1], x[:, 0:TQ - 1],
                                           ALU.mult, ALU.add)
                    at(v.tensor_scalar(xk_bf[ci][:, 0:1], x[:, 0:1], muk[:, ci:ci + 1],
                                       None, ALU.mult))
                else:
                    v.tensor_tensor(dtmp[:], x[:, g0:g0 + TQ], x[:, g0 - 1:g0 + TQ - 1],
                                    ALU.subtract)
                    v.scalar_tensor_tensor(xr_bf[ci][:], dtmp[:], mur[:, ci:ci + 1],
                                           x[:, g0 - 1:g0 + TQ - 1], ALU.mult, ALU.add)
                    at(v.scalar_tensor_tensor(xk_bf[ci][:], dtmp[:], muk[:, ci:ci + 1],
                                              x[:, g0 - 1:g0 + TQ - 1], ALU.mult, ALU.add))
            prog.emit("vector", fx)
        xr_all = ctx.vv.n

        if pend_ln is not None:
            pq, pwps = pend_ln
            lnr.xbf_free = [(ctx.vv, xr_all)]
            for ltq in range(pq * tpc, (pq + 1) * tpc):
                lnr.emit_tq(ltq, br_w=lambda ci, pwps=pwps: [pwps[ci]])
            pend_ln = None

        r_last = 0
        ka_last = 0
        war_kmx = []
        for which in ("r", "k"):
            wsb = wr_sb if which == "r" else wk_sb
            src = xr_bf if which == "r" else xk_bf
            for dj in range(8):
                for tc in range(nsub):
                    b, bwait = sl_mm.acquire(pjit)
                    at, mmc = ctx.mm.inc()

                    def fmm(t, b=b, bwait=bwait, dj=dj, tc=tc, wsb=wsb, src=src,
                            xc=xr_all, lw=_norm([lw]), at=at):
                        t.wait_ge(ctx.vv.h, xc)
                        wait_pairs(t, lw)
                        wait_pairs(t, bwait)
                        last = None
                        for ci in range(8):
                            last = t.matmul(ctx.bank(b), wt(wsb, ci, dj, 1024),
                                            src[ci][:, tc * 512:(tc + 1) * 512],
                                            start=(ci == 0), stop=(ci == 7))
                        at(last)
                    prog.emit("tensor", fmm)

                    if which == "r":
                        at, r_last = ctx.ss.inc()

                        def fev(s, b=b, c=mmc, dj=dj, tc=tc, at=at):
                            s.wait_ge(ctx.mm.h, c)
                            at(s.activation(
                                r_grp[:, dj * TQ + tc * 512: dj * TQ + (tc + 1) * 512],
                                ctx.bank(b), AF.Sigmoid, bias=ctx.zero_b[:, 0:1], scale=1.0))
                        prog.emit("scalar", fev)
                        sl_mm.release(pjit, [(ctx.ss, r_last)])
                    else:
                        at, kmc = ctx.vv.inc()

                        def fev(v, b=b, c=mmc, wk=list(war_kmx), at=at):
                            v.wait_ge(ctx.mm.h, c)
                            wait_pairs(v, wk)
                            at(v.tensor_scalar(kmx[:], ctx.bank(b), 0.0, None, ALU.max))
                        prog.emit("vector", fev)
                        sl_mm.release(pjit, [(ctx.vv, kmc)])

                        at, ka_last = ctx.ss.inc()

                        def fsq(s, c=kmc, dj=dj, tc=tc, at=at):
                            s.wait_ge(ctx.vv.h, c)
                            at(s.square(
                                ka_grp[:, dj * TQ + tc * 512: dj * TQ + (tc + 1) * 512],
                                kmx[:]))
                        prog.emit("scalar", fsq)
                        war_kmx = [(ctx.ss, ka_last)]
                    pjit += 1

        wps_tq = {}
        for dj in range(8):
            slot, swait = sl_br.acquire(tq * 8 + dj)
            evl = 0
            for tc in range(nsub):
                b, bwait = sl_mm.acquire(pjit)
                at, mmc = ctx.mm.inc()

                def fmm(t, b=b, bwait=bwait, dj=dj, tc=tc, kc=ka_last, at=at):
                    t.wait_ge(ctx.ss.h, kc)
                    wait_pairs(t, bwait)
                    last = None
                    for fi in range(8):
                        last = t.matmul(ctx.bank(b), wt(wv_sb, fi, dj, 1024),
                                        ka_grp[:, fi * TQ + tc * 512: fi * TQ + (tc + 1) * 512],
                                        start=(fi == 0), stop=(fi == 7))
                    at(last)
                prog.emit("tensor", fmm)

                at, evl = ctx.vv.inc()

                def fev(v, b=b, c=mmc, dj=dj, tc=tc, slot=slot, rc=r_last,
                        swait=(swait if tc == 0 else []), at=at):
                    v.wait_ge(ctx.mm.h, c)
                    v.wait_ge(ctx.ss.h, rc)
                    wait_pairs(v, swait)
                    at(v.tensor_tensor(
                        brst[slot][:, tc * 512:(tc + 1) * 512],
                        r_grp[:, dj * TQ + tc * 512: dj * TQ + (tc + 1) * 512],
                        ctx.bank(b), ALU.mult))
                prog.emit("vector", fev)
                sl_mm.release(pjit, [(ctx.vv, evl)])
                pjit += 1

            em, wp = ctx.Ls.issue()

            def fst(s, dj=dj, tq=tq, slot=slot, c=evl, em=em,
                    bf=(br_free if tq == 0 and dj == 0 else [])):
                s.wait_ge(ctx.vv.h, c)
                wait_pairs(s, bf)
                em(s, lambda: s.dma_start(
                    Br[dj * 128:(dj + 1) * 128, tq * TQ:(tq + 1) * TQ], brst[slot][:]))
            prog.emit("sync", fst)
            wps_tq[dj] = wp
            sl_br.release(tq * 8 + dj, [wp])
        pend_ln = (tq, wps_tq)
        war_xr = [(ctx.mm, ctx.mm.n)]

    lnr.psum03 = [(ctx.mm, ctx.mm.n), (ctx.vv, ctx.vv.n), (ctx.ss, ctx.ss.n)]
    if pend_ln is not None:
        pq, pwps = pend_ln
        lnr.xbf_free = [(ctx.vv, ctx.vv.n)]
        for ltq in range(pq * tpc, (pq + 1) * tpc):
            lnr.emit_tq(ltq, br_w=lambda ci, pwps=pwps: [pwps[ci]])
    return ctx.finals()



def alloc_moe_pre(ctx, stW, prog, Wg_d, Wu_d, Wd_d, Wgs_d, Wus_d, Wds_d, wfree):
    u = ctx.uniq()
    nc = ctx.nc
    pre = {}
    pre["wg"] = [stW.enter_context(nc.sbuf_tensor(u + f"wge{i}", [128, 6144], BF16))
                 for i in range(2)]
    pre["wu"] = [stW.enter_context(nc.sbuf_tensor(u + f"wue{i}", [128, 6144], BF16))
                 for i in range(2)]
    pre["wd"] = [stW.enter_context(nc.sbuf_tensor(u + f"wde{i}", [128, 6144], BF16))
                 for i in range(2)]
    order = [NE] + list(range(NE))
    pre["order"] = order
    pre["wslots"] = {}

    def emit_wload(eidx, wait):
        e = order[eidx]
        slot = eidx % 2
        if e == NE:
            g_d, u_d, d_d = Wgs_d, Wus_d, Wds_d
        else:
            g_d, u_d, d_d = Wg_d[e], Wu_d[e], Wd_d[e]
        w1 = load_w(ctx, prog, g_d, pre["wg"][slot], D, FF, wait=wait)
        w2 = load_w(ctx, prog, u_d, pre["wu"][slot], D, FF, wait=())
        w3 = load_w(ctx, prog, d_d, pre["wd"][slot], FF, D, wait=())
        pre["wslots"][eidx] = (slot, [w1, w2, w3])
    pre["emit_wload"] = emit_wload
    emit_wload(0, wfree)
    emit_wload(1, wfree)
    return pre


def emit_moe(ctx, st, prog, Rt_d, Bias_d, Wg_d, Wu_d, Wd_d, Wgs_d, Wus_d, Wds_d,
             Wrt_scratch, Br, S_in_d, ready, pre, lnr):
    u = ctx.uniq()
    nc, T = ctx.nc, ctx.T
    ntt = T // 128
    nq = T // 512
    xbf = ready["xbf"]
    psum_free = _norm(ready.get("psum", []))
    br_free = _norm(ready.get("br_free", []))
    sout = _norm(ready.get("sout", [[]])[-1])

    import contextlib as _cl
    wg_sb, wu_sb, wd_sb = pre["wg"], pre["wu"], pre["wd"]
    wrow = [st.enter_context(nc.sbuf_tensor(u + "wrow0", [1, 512], BF16))]
    sg_t = st.enter_context(nc.sbuf_tensor(u + "sgt", [128, 512], BF16))
    hp_sb = st.enter_context(nc.sbuf_tensor(u + "hps", [128, 3072], BF16))
    brmo = [st.enter_context(nc.sbuf_tensor(u + f"brmo{i}", [128, 2048], F32))
            for i in range(2)]

    order = pre["order"]
    wslots = pre["wslots"]
    emit_wload = pre["emit_wload"]

    sl_g = Slot(2)
    sl_u = Slot(2)
    sl_wb = Slot(1)
    sl_d = Slot(2)
    sl_bm = Slot(2)
    for s_ in (sl_g, sl_u, sl_d):
        for i in range(s_.n):
            s_.waits[i] = list(psum_free)
    psum47_free = psum_free + _norm(ready.get("psum47", []))
    sl_wb.waits[0] = list(psum47_free)
    for i in range(sl_d.n):
        sl_d.waits[i] = list(psum47_free)
    state = {"war_sg": [], "war_hp": [], "war_wrow": [], "wrt_wp": [],
             "war_wrow2": [[], []], "writ": 0,
             "git": 0, "uit": 0, "wbit": 0, "dit": 0}
    acc_last = [[] for _ in range(nq)]
    acc_prev = {(tq, h): [] for tq in range(nq) for h in range(2)}

    def expert_body(eidx):
        e = order[eidx]
        slot, wld = wslots[eidx]
        shared = (e == NE)


        for tq in range(nq):
            wb_c = 0
            if not shared:
                wsl = 0
                em, wrow_wp = ctx.Ls.issue()

                def fwr(s, e=e, tq=tq, wsl=wsl,
                        c=(_norm(state["wrt_wp"]) if tq == 0 else []),
                        ww=_norm(state["war_wrow2"][wsl]), em=em):
                    wait_pairs(s, c)
                    wait_pairs(s, ww)
                    em(s, lambda: s.dma_start(
                        wrow[wsl][:], Wrt_scratch[e:e + 1, tq * 512:(tq + 1) * 512]))
                prog.emit("sync", fwr)

                b_wb, wbwait = sl_wb.acquire(state["wbit"])
                at, wb_c = ctx.mm.inc()

                def fwb(t, wsl=wsl, wbwait=wbwait, c=_norm([wrow_wp]), at=at):
                    wait_pairs(t, c)
                    wait_pairs(t, wbwait)
                    at(t.matmul(ctx.bank(4), ctx.allones_bf[0:1, :],
                                wrow[wsl][:], start=True, stop=True))
                prog.emit("tensor", fwb)
                state["war_wrow2"][wsl] = [(ctx.mm, wb_c)]
                state["writ"] += 1

            h_last = 0
            for fj in range(6):
                bg, gwait = sl_g.acquire(state["git"])
                at, g_c = ctx.mm.inc()

                def fg(t, bg=bg, gwait=gwait, fj=fj, tq=tq, slot=slot,
                       wld=_norm(wld), xw=(_norm(xbf[tq]) if fj == 0 else []), at=at):
                    wait_pairs(t, wld)
                    wait_pairs(t, xw)
                    wait_pairs(t, gwait)
                    last = None
                    for ci in range(8):
                        last = t.matmul(ctx.bank(bg), wt(wg_sb[slot], ci, fj, FF),
                                        ctx.x_bf[ci][:, tq * 512:(tq + 1) * 512],
                                        start=(ci == 0), stop=(ci == 7))
                    at(last)
                prog.emit("tensor", fg)

                at, sil_c = ctx.ss.inc()

                def fsl(s, bg=bg, c=g_c, wsg=list(state["war_sg"]), at=at):
                    s.wait_ge(ctx.mm.h, c)
                    wait_pairs(s, wsg)
                    at(s.activation(sg_t[:], ctx.bank(bg), AF.Silu,
                                    bias=ctx.zero_b[:, 0:1], scale=1.0))
                prog.emit("scalar", fsl)
                sl_g.release(state["git"], [(ctx.ss, sil_c)])
                state["git"] += 1

                bu, uwait = sl_u.acquire(state["uit"])
                at, u_c = ctx.mm.inc()

                def fu(t, bu=bu, uwait=uwait, fj=fj, tq=tq, slot=slot, at=at):
                    wait_pairs(t, uwait)
                    last = None
                    for ci in range(8):
                        last = t.matmul(ctx.bank(2 + bu), wt(wu_sb[slot], ci, fj, FF),
                                        ctx.x_bf[ci][:, tq * 512:(tq + 1) * 512],
                                        start=(ci == 0), stop=(ci == 7))
                    at(last)
                prog.emit("tensor", fu)

                atf = ctx.vv.inc() if not shared else None
                at, h_last = ctx.vv.inc()

                def fh(v, bu=bu, fj=fj, c1=sil_c, c2=u_c, shared=shared,
                       whp=list(state["war_hp"]), at=at, atf=atf):
                    v.wait_ge(ctx.ss.h, c1)
                    v.wait_ge(ctx.mm.h, c2)
                    wait_pairs(v, whp)
                    dst = hp_sb[:, fj * 512:(fj + 1) * 512]
                    if not shared:
                        atf[0](v.tensor_tensor(dst, sg_t[:], ctx.bank(2 + bu), ALU.mult))
                        v.wait_ge(ctx.vv.h, atf[1])
                        at(v.tensor_tensor(dst, dst, ctx.bank(4), ALU.mult))
                    else:
                        at(v.tensor_tensor(dst, sg_t[:], ctx.bank(2 + bu), ALU.mult))
                prog.emit("vector", fh)
                state["war_sg"] = [(ctx.vv, h_last)]
                sl_u.release(state["uit"], [(ctx.vv, h_last)])
                state["uit"] += 1
            if not shared:
                sl_wb.release(state["wbit"], [(ctx.vv, h_last)])
                state["wbit"] += 1

            ev_last = 0
            for half in range(2):
                bsl, bswait = sl_bm.acquire((eidx * nq + tq) * 2 + half)
                for dj4 in range(4):
                    dj = half * 4 + dj4
                    bd, dwait = sl_d.acquire(state["dit"])
                    at, d_c = ctx.mm.inc()

                    def fd(t, bd=bd, dwait=dwait, dj=dj, slot=slot, c=h_last, at=at):
                        t.wait_ge(ctx.vv.h, c)
                        wait_pairs(t, dwait)
                        last = None
                        for fi in range(6):
                            last = t.matmul(ctx.bank(5 + bd), wt(wd_sb[slot], fi, dj, D),
                                            hp_sb[:, fi * 512:(fi + 1) * 512],
                                            start=(fi == 0), stop=(fi == 5))
                        at(last)
                    prog.emit("tensor", fd)

                    at, ev_last = ctx.vv.inc()

                    def fde(v, bd=bd, dj4=dj4, c=d_c, bsl=bsl,
                            bswait=(bswait if dj4 == 0 else []), at=at):
                        v.wait_ge(ctx.mm.h, c)
                        wait_pairs(v, bswait)
                        at(v.tensor_copy(brmo[bsl][:, dj4 * 512:(dj4 + 1) * 512],
                                         ctx.bank(5 + bd)))
                    prog.emit("vector", fde)
                    sl_d.release(state["dit"], [(ctx.vv, ev_last)])
                    state["dit"] += 1

                out_ap = Br[half * 512:(half + 1) * 512,
                            tq * 512:(tq + 1) * 512].rearrange("(dj p) c -> p dj c", p=128)
                in_ap = brmo[bsl][:].rearrange("p (dj c) -> p dj c", dj=4)
                em, acc_wp = ctx.Lg.issue()

                def fac(g, c=ev_last, prev=_norm(acc_prev[(tq, half)]), shared=shared,
                        out_ap=out_ap, in_ap=in_ap, em=em,
                        bf=(br_free if shared else [])):
                    g.wait_ge(ctx.vv.h, c)
                    wait_pairs(g, bf)
                    wait_pairs(g, prev)
                    if shared:
                        em(g, lambda: g.dma_start(out_ap, in_ap))
                    else:
                        em(g, lambda: g.dma_start(out_ap, in_ap, accum_op=ALU.add))
                prog.emit("gpsimd", fac)
                acc_prev[(tq, half)] = [acc_wp]
                sl_bm.release((eidx * nq + tq) * 2 + half, [acc_wp])
            acc_last[tq] = acc_prev[(tq, 0)] + acc_prev[(tq, 1)]
            state["war_hp"] = [(ctx.mm, ctx.mm.n)]
            if not shared:
                state["war_wrow"] = [(ctx.mm, wb_c)]

        if eidx + 2 < len(order):
            emit_wload(eidx + 2, [(ctx.mm, ctx.mm.n)])

    expert_body(0)

    stR = st.enter_context(_cl.ExitStack())
    rt_sb = stR.enter_context(nc.sbuf_tensor(u + "rt", [128, 64], F32))
    xf32 = stR.enter_context(nc.sbuf_tensor(u + "xf", [128, min(1024, T)], F32))
    s_tm = stR.enter_context(nc.sbuf_tensor(u + "stm", [128, ntt * 8], F32))
    sp_tm = stR.enter_context(nc.sbuf_tensor(u + "sptm", [128, ntt * 8], F32))
    m1t = stR.enter_context(nc.sbuf_tensor(u + "m1t", [128, ntt], F32))
    m2t = stR.enter_context(nc.sbuf_tensor(u + "m2t", [128, ntt], F32))
    et = stR.enter_context(nc.sbuf_tensor(u + "et", [128, ntt * 8], F32))
    wmt = stR.enter_context(nc.sbuf_tensor(u + "wmt", [128, ntt * 8], F32))
    brow = stR.enter_context(nc.sbuf_tensor(u + "brow", [1, ntt * 8], F32))
    wfm = [stR.enter_context(nc.sbuf_tensor(u + f"wfm{i}", [8, 128], BF16))
           for i in range(2)]

    em, lrt_wp = ctx.Ls.issue()

    def flrt(s, em=em):
        em(s, lambda: s.dma_start(rt_sb[:].rearrange("p (ci d) -> p ci d", ci=8),
                                  Rt_d.rearrange("(ci p) d -> p ci d", p=128)))
    prog.emit("sync", flrt)
    _rt_mm_base = ctx.mm.n
    XW = min(1024, T)
    nxh = T // XW
    ngrp = 8 * nxh
    xl_wps = [None] * ngrp
    for gi in range(ngrp):
        ci, h2 = gi // nxh, gi % nxh
        em, xl_wps[gi] = ctx.Ls.issue()

        def fxl(s, ci=ci, h2=h2, gi=gi, em=em, so=(sout if gi == 0 else [])):
            wait_pairs(s, so)
            if gi >= 1:
                s.wait_ge(ctx.mm.h, _rt_mm_base + gi)
            em(s, lambda: s.dma_start(
                xf32[:], S_in_d[ci * 128:(ci + 1) * 128, h2 * XW:(h2 + 1) * XW]))
        prog.emit("sync", fxl)

    brow_wps = []
    for r in range(ntt):
        em, wp = ctx.Ls.issue()
        brow_wps.append(wp)

        def fb(s, r=r, em=em):
            em(s, lambda: s.dma_start(brow[0:1, r * 8:(r + 1) * 8], Bias_d[0:1, 0:8]))
        prog.emit("sync", fb)

    ats = [ctx.mm.inc() for _ in range(ngrp)]
    lg_c = ats[-1][1]
    shared_readers = [(ctx.ss, ctx.ss.n), (ctx.vv, ctx.vv.n)]
    ntt_h = XW // 128

    def frt(t, lrt=_norm([lrt_wp]), ats=ats, xl=[_norm([w]) for w in xl_wps],
            pf=list(psum_free) + _norm(shared_readers)):
        wait_pairs(t, pf)
        wait_pairs(t, lrt)
        for gi in range(ngrp):
            ci, h2 = gi // nxh, gi % nxh
            wait_pairs(t, xl[gi])
            last = None
            for tl in range(ntt_h):
                tt = h2 * ntt_h + tl
                last = t.matmul(ctx.pball[:, tt * 8:(tt + 1) * 8],
                                xf32[:, tl * 128:(tl + 1) * 128],
                                rt_sb[:, ci * 8:(ci + 1) * 8],
                                start=(gi == 0 and tl == 0),
                                stop=(gi == ngrp - 1 and tl == ntt_h - 1),
                                skip_group_check=True)
            ats[gi][0](last)
    prog.emit("tensor", frt)

    at, sg_c = ctx.ss.inc()

    def fsg(s, c=lg_c, at=at):
        s.wait_ge(ctx.mm.h, c)
        at(s.activation(s_tm[:], ctx.pball[:, 0:ntt * 8], AF.Sigmoid,
                        bias=ctx.zero_b[:, 0:1], scale=1.0))
    prog.emit("scalar", fsg)

    nbc = (ntt * 8 + 511) // 512
    at, bb_c = ctx.mm.inc()

    def fbb(t, c=_norm(brow_wps), sgc=sg_c, at=at):
        wait_pairs(t, c)
        t.wait_ge(ctx.ss.h, sgc)
        last = None
        for j in range(nbc):
            w = min(512, ntt * 8 - j * 512)
            last = t.matmul(ctx.pball[:, j * 512: j * 512 + w], ctx.ones_f32[0:1, :],
                            brow[0:1, j * 512: j * 512 + w], start=True, stop=True)
        at(last)
    prog.emit("tensor", fbb)

    fence_counts = [ctx.vv.inc() for _ in range(10)]
    at, w_c = ctx.vv.inc()

    def froute(v, c=bb_c, at=at, fences=fence_counts):
        def fence(i, instr):
            fences[i][0](instr)
            v.wait_ge(ctx.vv.h, fences[i][1])
        v.wait_ge(ctx.mm.h, c)
        fence(0, v.tensor_copy(et[:], ctx.pball[:, 0:ntt * 8]))
        fence(1, v.tensor_tensor(sp_tm[:], s_tm[:], et[:], ALU.add))
        fence(2, v.tensor_reduce(m1t[:], sp_tm[:].rearrange("p (g j) -> p g j", j=8),
                                 AX.X, ALU.max))
        last = None
        for j in range(8):
            last = v.tensor_tensor(et[:].rearrange("p (g j) -> p g j", j=8)[:, :, j],
                                   sp_tm[:].rearrange("p (g j) -> p g j", j=8)[:, :, j],
                                   m1t[:], ALU.is_ge)
        fence(3, last)
        fence(4, v.scalar_tensor_tensor(et[:], et[:], -BIG, sp_tm[:], ALU.mult, ALU.add))
        fence(5, v.tensor_reduce(m2t[:], et[:].rearrange("p (g j) -> p g j", j=8),
                                 AX.X, ALU.max))
        last = None
        for j in range(8):
            last = v.tensor_tensor(wmt[:].rearrange("p (g j) -> p g j", j=8)[:, :, j],
                                   sp_tm[:].rearrange("p (g j) -> p g j", j=8)[:, :, j],
                                   m2t[:], ALU.is_ge)
        fence(6, last)
        fence(7, v.tensor_tensor(wmt[:], wmt[:], s_tm[:], ALU.mult))
        fence(8, v.tensor_reduce(m1t[:], wmt[:].rearrange("p (g j) -> p g j", j=8),
                                 AX.X, ALU.add))
        v.tensor_scalar(m1t[:], m1t[:], EPS_MOE, None, ALU.add)
        v.wait_ge(ctx.vv.h, fences[8][1])
        fence(9, v.reciprocal(m2t[:], m1t[:]))
        last = None
        for j in range(8):
            last = v.tensor_tensor(wmt[:].rearrange("p (g j) -> p g j", j=8)[:, :, j],
                                   wmt[:].rearrange("p (g j) -> p g j", j=8)[:, :, j],
                                   m2t[:], ALU.mult)
        at(last)
    prog.emit("vector", froute)

    tr_ev = [0] * ntt
    wfm_war = [[], []]
    for tt in range(ntt):
        b = tt % 2
        at, trc = ctx.mm.inc()

        def ftr(t, tt=tt, b=b, c=w_c, prev=(tr_ev[tt - 2] if tt >= 2 else 0), at=at):
            if tt == 0:
                t.wait_ge(ctx.vv.h, c)
                wait_pairs(t, _norm([ctx.consts_l]))
            if prev:
                t.wait_ge(ctx.vv.h, prev)
            at(t.transpose(ctx.bank(b, p1=8, w=128), wmt[:, tt * 8:(tt + 1) * 8],
                           ctx.ident[:]))
        prog.emit("tensor", ftr)

        at, tec = ctx.vv.inc()

        def fte(v, tt=tt, b=b, c=trc, wfw=_norm(wfm_war[tt % 2]), at=at):
            v.wait_ge(ctx.mm.h, c)
            wait_pairs(v, wfw)
            at(v.tensor_copy(wfm[tt % 2][:], ctx.bank(b, p1=8, w=128)))
        prog.emit("vector", fte)
        tr_ev[tt] = tec

        emw, wfw_wp = ctx.Ls.issue()

        def fwd(s, tt=tt, c=tec, emw=emw):
            s.wait_ge(ctx.vv.h, c)
            emw(s, lambda: s.dma_start(
                Wrt_scratch[0:8, tt * 128:(tt + 1) * 128], wfm[tt % 2][:]))
        prog.emit("sync", fwd)
        wfm_war[tt % 2] = [wfw_wp]
    state["wrt_wp"] = ctx.Ls.finals()

    router_readers = [(ctx.vv, ctx.vv.n), (ctx.ss, sg_c)]
    for s_ in (sl_g, sl_u):
        for i in range(s_.n):
            s_.waits[i] = s_.waits[i] + _norm(router_readers)

    for eidx in range(1, len(order)):
        expert_body(eidx)

    lnr.br_default = lambda ci, tq: list(acc_last[tq])
    lnr.psum03 = [(ctx.mm, ctx.mm.n), (ctx.vv, ctx.vv.n), (ctx.ss, ctx.ss.n)]
    lnr.psum47 = [(ctx.mm, ctx.mm.n), (ctx.vv, ctx.vv.n)]
    return ctx.finals()



DEBUG_OUTS = set()


def build_program(T, blocks):
    nc = bass.Bass("TRN2", target_bir_lowering=False, detect_race_conditions=False)
    dbg = lambda n: ("ExternalOutput" if n in DEBUG_OUTS else "Internal")
    S_in = nc.dram_tensor("S_in", [D, T], F32, kind="ExternalInput").ap()
    S_out = nc.dram_tensor("S_out", [D, T], F32, kind="ExternalOutput").ap()
    ident = nc.dram_tensor("ident", [128, 128], F32, kind="ExternalInput").ap()
    br_kind = "ExternalInput" if "lnonly" in blocks else dbg("Br_scr")
    Br = nc.dram_tensor("Br_scr", [D, T], F32, kind=br_kind).ap()
    O_spill = nc.dram_tensor("O_scr", [D, T], BF16, kind="Internal").ap()
    Smid = [nc.dram_tensor(f"Smid{i}", [D, T], F32, kind="Internal").ap() for i in range(3)]
    Wrt = nc.dram_tensor("Wrt_scr", [8, T], BF16, kind=dbg("Wrt_scr")).ap()

    wd = {}
    if "attn0" in blocks:
        for n in ("l0_wqT", "l0_wkT", "l0_wvT", "l0_woT"):
            wd[n] = nc.dram_tensor(n, [D, D], F32, kind="ExternalInput").ap()
    if "ffn0" in blocks:
        for n in ("f0_wrT", "f0_wkT", "f0_wvT"):
            wd[n] = nc.dram_tensor(n, [D, D], F32, kind="ExternalInput").ap()
        wd["f0_mur"] = nc.dram_tensor("f0_mur", [D, 1], F32, kind="ExternalInput").ap()
        wd["f0_muk"] = nc.dram_tensor("f0_muk", [D, 1], F32, kind="ExternalInput").ap()
    if "attn1" in blocks:
        for n in ("l1_wq", "l1_wk", "l1_wv", "l1_wo"):
            wd[n] = nc.dram_tensor(n, [D, D], F32, kind="ExternalInput").ap()
    if "moe" in blocks:
        wd["m_rt"] = nc.dram_tensor("m_rt", [D, 8], F32, kind="ExternalInput").ap()
        wd["m_bias"] = nc.dram_tensor("m_bias", [1, 8], F32, kind="ExternalInput").ap()
        for e in range(NE):
            wd[f"m_wg{e}"] = nc.dram_tensor(f"m_wg{e}", [D, FF], F32, kind="ExternalInput").ap()
            wd[f"m_wu{e}"] = nc.dram_tensor(f"m_wu{e}", [D, FF], F32, kind="ExternalInput").ap()
            wd[f"m_wd{e}"] = nc.dram_tensor(f"m_wd{e}", [FF, D], F32, kind="ExternalInput").ap()
        wd["m_wgs"] = nc.dram_tensor("m_wgs", [D, FF], F32, kind="ExternalInput").ap()
        wd["m_wus"] = nc.dram_tensor("m_wus", [D, FF], F32, kind="ExternalInput").ap()
        wd["m_wds"] = nc.dram_tensor("m_wds", [FF, D], F32, kind="ExternalInput").ap()

    ctx = Ctx(nc, T)
    prog = Prog(nc)
    ctx.init_consts(prog, ident)

    import contextlib as _cl
    stream_in = S_in
    entry_wps = emit_entry(ctx, prog, stream_in)
    ready = {"all": list(entry_wps), "xbf": [list(entry_wps)] * (T // 512),
             "sout": [[]], "br_free": [], "psum": [], "psum47": []}

    def prep(bi, stW, wfree):
        blk = blocks[bi]
        if blk == "attn0":
            return alloc_attn_pre(ctx, stW, prog, wd["l0_wqT"], wd["l0_wkT"],
                                  wd["l0_wvT"], wfree)
        if blk == "attn1":
            return alloc_attn_pre(ctx, stW, prog, wd["l1_wq"], wd["l1_wk"],
                                  wd["l1_wv"], wfree)
        if blk == "ffn0":
            return alloc_ffn0_pre(ctx, stW, prog, wd["f0_wrT"], wd["f0_wkT"],
                                  wd["f0_wvT"], wd["f0_mur"], wd["f0_muk"], wfree)
        if blk == "moe":
            return alloc_moe_pre(ctx, stW, prog,
                                 [wd[f"m_wg{e}"] for e in range(NE)],
                                 [wd[f"m_wu{e}"] for e in range(NE)],
                                 [wd[f"m_wd{e}"] for e in range(NE)],
                                 wd["m_wgs"], wd["m_wus"], wd["m_wds"], wfree)
        if blk == "lnonly":
            return None
        raise ValueError(blk)

    stW = _cl.ExitStack()
    pre = prep(0, stW, wfree=[])

    for bi, blk in enumerate(blocks):
        is_last = (bi == len(blocks) - 1)
        s_out = S_out if is_last else Smid[bi]
        rs = 2.0 if blk == "attn1" else 1.0
        lnr = ResLN(ctx, prog, stream_in, Br, s_out, rs,
                    next_sbuf=(not is_last), ready=ready)
        with _cl.ExitStack() as st:
            if blk in ("attn0", "attn1"):
                wo = wd["l0_woT"] if blk == "attn0" else wd["l1_wo"]
                emit_attn(ctx, st, prog, wo, Br, O_spill,
                          gelu=(blk == "attn1"), ready=ready, pre=pre, lnr=lnr)
            elif blk == "ffn0":
                emit_ffn0(ctx, st, prog, Br, ready=ready, pre=pre, lnr=lnr)
            elif blk == "moe":
                emit_moe(ctx, st, prog, wd["m_rt"], wd["m_bias"],
                         [wd[f"m_wg{e}"] for e in range(NE)],
                         [wd[f"m_wu{e}"] for e in range(NE)],
                         [wd[f"m_wd{e}"] for e in range(NE)],
                         wd["m_wgs"], wd["m_wus"], wd["m_wds"], Wrt, Br,
                         stream_in, ready=ready, pre=pre, lnr=lnr)
            elif blk == "lnonly":
                pass
            else:
                raise ValueError(blk)
        stW.close()
        nstW = _cl.ExitStack()
        npre = prep(bi + 1, nstW, wfree=ctx.finals()) if bi + 1 < len(blocks) else None
        ready = lnr.finish()
        stW, pre = nstW, npre
        prog.flush(blk)
        stream_in = s_out
    stW.close()

    def ffin(s, fin=_norm(ctx.Ls.finals() + ctx.Lg.finals())):
        wait_pairs(s, fin)
    prog.emit("sync", ffin)
    prog.flush("fin")
    return nc



_IDENT = np.eye(128, dtype=np.float32)


def _prep_weights(kw):
    t = lambda a: np.ascontiguousarray(np.asarray(a).T)
    c = lambda a: np.ascontiguousarray(np.asarray(a))
    w = {}
    w["l0_wqT"] = t(kw["L0_attn_w_q"])
    w["l0_wkT"] = t(kw["L0_attn_w_k"])
    w["l0_wvT"] = t(kw["L0_attn_w_v"])
    w["l0_woT"] = t(kw["L0_attn_w_o"])
    w["f0_wrT"] = t(kw["L0_ffn_W_r"])
    w["f0_wkT"] = t(kw["L0_ffn_W_k"])
    w["f0_wvT"] = t(kw["L0_ffn_W_v"])
    w["f0_mur"] = c(kw["L0_ffn_mu_r"]).reshape(D, 1)
    w["f0_muk"] = c(kw["L0_ffn_mu_k"]).reshape(D, 1)
    w["l1_wq"] = c(kw["L1_attn_Wq"])
    w["l1_wk"] = c(kw["L1_attn_Wk"])
    w["l1_wv"] = c(kw["L1_attn_Wv"])
    w["l1_wo"] = c(kw["L1_attn_Wpost"])
    w["m_rt"] = c(kw["L1_ffn_W_gate_router"])
    w["m_bias"] = c(kw["L1_ffn_expert_bias"]).reshape(1, 8)
    for e in range(NE):
        w[f"m_wg{e}"] = c(kw["L1_ffn_Wg"][e])
        w[f"m_wu{e}"] = c(kw["L1_ffn_Wu"][e])
        w[f"m_wd{e}"] = c(kw["L1_ffn_Wd"][e])
    w["m_wgs"] = c(kw["L1_ffn_Wg_s"])
    w["m_wus"] = c(kw["L1_ffn_Wu_s"])
    w["m_wds"] = c(kw["L1_ffn_Wd_s"])
    w["ident"] = _IDENT
    return w


_BLOCK_WSETS = {
    "attn0": ["l0_wqT", "l0_wkT", "l0_wvT", "l0_woT"],
    "ffn0": ["f0_wrT", "f0_wkT", "f0_wvT", "f0_mur", "f0_muk"],
    "attn1": ["l1_wq", "l1_wk", "l1_wv", "l1_wo"],
    "moe": (["m_rt", "m_bias", "m_wgs", "m_wus", "m_wds"]
            + [f"m_w{k}{e}" for e in range(NE) for k in ("g", "u", "d")]),
}


def kernel(**inputs):
    T = 4096
    x = np.asarray(inputs["L0_attn_x"])
    B = x.shape[0]
    w = _prep_weights(inputs)

    stages = [["attn0", "ffn0", "attn1", "moe"]]
    S = [np.ascontiguousarray(x[b].T) for b in range(B)]
    for blocks in stages:
        nc = build_program(T, blocks)
        names = ["ident"] + sum((_BLOCK_WSETS[b] for b in blocks), [])
        in_maps = []
        for b in range(B):
            m = {"S_in": S[b]}
            for n in names:
                m[n] = w[n]
            in_maps.append(m)
        res = run_bass_kernel_spmd(nc, in_maps, core_ids=list(range(B)))
        S = [res.results[b]["S_out"] for b in range(B)]
    out = np.stack([S[b].T for b in range(B)], axis=0).astype(np.float32)
    return out
